# version 31
# baseline (speedup 1.0000x reference)
import numpy as np
from contextlib import ExitStack

import concourse.bass as bass
import concourse.mybir as mybir
from concourse.bass_utils import run_bass_kernel_spmd

F32 = mybir.dt.float32
BF16 = mybir.dt.bfloat16
AF = mybir.ActivationFunctionType
ALU = mybir.AluOpType

NCORES = 8
B_LOC = 2
S = 2048
D = 1024
EXP_SCALE = 0.125
EXP_BIAS = -10.0
LN_EPS = 1e-5
NTT = 16
NSB = 4
NFT = 4
NCY = 3


class Sem:
    def __init__(self, nc, es, name):
        self.h = es.enter_context(nc.semaphore(name))
        self.n = 0

    def inc(self, k=1):
        self.n += k
        return self.n


DEBUG = [False]


def build_nc():
    nc = bass.Bass("TRN2", target_bir_lowering=False)
    es = ExitStack()

    dx = nc.dram_tensor("x", [B_LOC * S, D], F32, kind="ExternalInput")
    dex4 = nc.dram_tensor("ex4", [512, 2048], F32, kind="ExternalInput")
    dout = nc.dram_tensor("out", [B_LOC * S, D], F32, kind="ExternalOutput")
    dbg_x1 = nc.dram_tensor("dbg_x1", [S, D], F32, kind="ExternalOutput")
    dbg_ao = nc.dram_tensor("dbg_ao", [128, 16384], BF16, kind="ExternalOutput")
    dwq = nc.dram_tensor("wqT", [D, D], F32, kind="ExternalInput")
    dwk = nc.dram_tensor("wkT", [D, D], F32, kind="ExternalInput")
    dwv = nc.dram_tensor("wvT", [D, D], F32, kind="ExternalInput")
    dwfc = nc.dram_tensor("wfcT", [D, D], F32, kind="ExternalInput")
    dcw1 = nc.dram_tensor("cw1", [D, D], F32, kind="ExternalInput")
    dcw2 = nc.dram_tensor("cw2", [D, 512], F32, kind="ExternalInput")
    dcw3 = nc.dram_tensor("cw3", [512, 256], F32, kind="ExternalInput")
    dcw4 = nc.dram_tensor("cw4", [256, 1], F32, kind="ExternalInput")
    dcb1 = nc.dram_tensor("cb1", [D], F32, kind="ExternalInput")
    dcb2 = nc.dram_tensor("cb2", [512], F32, kind="ExternalInput")
    dcb3 = nc.dram_tensor("cb3", [256], F32, kind="ExternalInput")
    dcb4 = nc.dram_tensor("cb4", [1], F32, kind="ExternalInput")
    dgw1 = nc.dram_tensor("gw1", [D, 512], F32, kind="ExternalInput")
    dgw2 = nc.dram_tensor("gw2", [512, 256], F32, kind="ExternalInput")
    dgw3 = nc.dram_tensor("gw3", [256, 1], F32, kind="ExternalInput")
    dgb1 = nc.dram_tensor("gb1", [512], F32, kind="ExternalInput")
    dgb2 = nc.dram_tensor("gb2", [256], F32, kind="ExternalInput")
    dgb3 = nc.dram_tensor("gb3", [1], F32, kind="ExternalInput")
    dar = nc.dram_tensor("ar", [1], F32, kind="ExternalInput")

    BASE = (nc.sbuf_base + 63) // 64 * 64
    TOP = nc.sbuf_top
    off = [0]

    def at(name, shape, dtype, align=64):
        o = (off[0] + align - 1) // align * align
        t = nc.alloc_sbuf_tensor_at(name, shape, dtype, offset=BASE + o)
        nbytes = 1
        for s_ in shape[1:]:
            nbytes *= s_
        nbytes *= mybir.dt.size(dtype)
        off[0] = o + nbytes
        return t

    def at_fixed(name, shape, dtype, o):
        return nc.alloc_sbuf_tensor_at(name, shape, dtype, offset=BASE + o)

    onesb = at("onesb", [128, 128], BF16)
    onesf = at("onesf", [1, 128], F32)
    stats = at("stats", [128, 12], F32)
    aggr = at("aggr", [128, 2], F32)
    aggr2 = at("aggr2", [128, 2], F32)
    lncol = at("lncol", [128, 4], F32)
    ccols = at("ccols", [128, 4], F32)
    c4 = at("c4", [128, 4], F32)
    negc4 = at("negc4", [128, 4], F32)
    tanh4 = at("tanh4", [128, 4], F32)
    gcT = at("gcT", [128, 8], BF16)
    g1T = at("g1T", [128, 4], BF16)
    g2T = at("g2T", [128, 2], BF16)
    arow = at("arow", [1, 8], F32)
    abrow = at("abrow", [1, 4], F32)
    g3pre = at("g3pre", [1, 1], F32)
    gtanh = at("gtanh", [1, 1], F32)
    expb = at("expb", [128, 1], F32)
    epscol = at("epscol", [128, 1], F32)
    rcol = at("rcol", [128, 2], F32)
    iwork = at("iwork", [128, 128], F32)
    iwork2 = at("iwork2", [128, 128], F32)
    indt = at("indt", [128, 16 * 32], BF16)
    expf4_0 = at("expf4_0", [128, 16 * 128], BF16)
    expf4_3 = at("expf4_3", [128, 16 * 128], BF16)
    redda = at("redda", [128, 128], BF16)
    reddb = at("reddb", [128, 128], BF16)
    rind = at("rind", [128, 8], BF16)
    e8 = at("e8", [128, 128], F32)
    gw3sb = at("gw3sb", [128, 2], BF16)
    lnb = at("lnb", [8, 512], F32)
    idf32 = at("idf32", [128, 128], F32)
    onec = at("onec", [128, 1], F32)
    tanhrow = at("tanhrow", [1, 256], F32)
    c4row = at("c4row", [1, 256], F32)
    assert off[0] <= 18000, off[0]

    off[0] = 18048
    cw1sb = at("cw1sb", [128, 64 * 128], BF16)
    cw2sb = at("cw2sb", [128, 32 * 128], BF16)
    cw3sb = at("cw3sb", [128, 8 * 128], BF16)
    cw4sb = at("cw4sb", [128, 2], BF16)
    cb1sb = at("cb1sb", [128, 8], F32)
    cb2sb = at("cb2sb", [128, 4], F32)
    cb3sb = at("cb3sb", [128, 2], F32)
    gb1row = at("gb1row", [1, 512], F32)
    gb2row = at("gb2row", [1, 256], F32)
    GW_OFF = (off[0] + 63) // 64 * 64
    off[0] = GW_OFF
    gw1sb = at("gw1sb", [128, 8 * 512], BF16)
    gw2sb = at("gw2sb", [128, 4 * 256], BF16)
    sumtile = at_fixed("sumtile", [128, 1024], F32, GW_OFF)
    sumtile2 = at_fixed("sumtile2", [128, 1024], F32, GW_OFF + 4096)
    expf4_1 = at_fixed("expf4_1", [128, 16 * 128], BF16, GW_OFF)
    expf4_2 = at_fixed("expf4_2", [128, 16 * 128], BF16, GW_OFF + 4096)
    x1bfgc = at_fixed("x1bfgc", [128, 1024], BF16, GW_OFF + 8192)
    WQ_OFF = (off[0] + 63) // 64 * 64
    off[0] = WQ_OFF
    wqsb = at("wqsb", [128, 64 * 128], BF16)
    wksb = at("wksb", [128, 64 * 128], BF16)
    wvsb = at("wvsb", [128, 64 * 128], BF16)
    x2slab = at_fixed("x2slab", [128, 2 * 1024], F32, WQ_OFF)
    xstg = at_fixed("xstg", [128, 1024], BF16, WQ_OFF + 8192)
    outbf = at_fixed("outbf", [128, 1024], BF16, WQ_OFF + 10240)
    outbf2 = at_fixed("outbf2", [128, 1024], BF16, WQ_OFF + 12288)
    WK_OFF = WQ_OFF + 16384
    x0f32 = at_fixed("x0f32", [128, 2 * 1024], F32, WK_OFF)
    h2T = at_fixed("h2T", [128, 4 * 512], BF16, WK_OFF + 8192)
    h3T = at_fixed("h3T", [128, 2 * 512], BF16, WK_OFF + 12288)
    WV_OFF = WK_OFF + 16384
    wfcsb = at_fixed("wfcsb", [128, 8 * 1024], BF16, WV_OFF)

    OV_OFF = (off[0] + 63) // 64 * 64
    off[0] = OV_OFF
    x0tok = at("x0tok", [128, 2 * 1024], BF16)
    x0T = at("x0T", [128, 2 * 8 * 512], BF16)
    qtsb = at("qtsb", [128, 8 * 512], BF16)
    ktsb = at("ktsb", [128, 8 * 512], BF16)
    vtsb = at("vtsb", [128, 8 * 512], BF16)
    kr = at("kr", [128, 2 * 512], BF16)
    prodb = at("prodb", [128, 2 * 2048], BF16)
    praw = at("praw", [128, 1024], BF16)
    pfm = at("pfm", [128, 1024], BF16)
    pexpsb = at("pexpsb", [128, 4 * 512], BF16)
    prdsb = at("prdsb", [128, 4 * 512], BF16)
    rcpf1 = at("rcpf1", [128, 512], F32)
    rcpf2 = at("rcpf2", [128, 512], F32)
    OVC_END = off[0]
    x1 = at_fixed("x1", [128, 16 * 1024], F32, OV_OFF)
    X1_END = OV_OFF + 16 * 1024 * 4

    AO_OFF = (max(OVC_END, X1_END) + 63) // 64 * 64
    off[0] = AO_OFF
    attnoutT = at("attnoutT", [128, 8 * 2048], BF16)
    AO_END = off[0]
    assert X1_END <= AO_OFF, (X1_END, AO_OFF)
    OT_OFF = AO_OFF
    outT0 = at_fixed("outT0", [128, 4 * 1024], F32, OT_OFF)
    outT1 = at_fixed("outT1", [128, 4 * 1024], F32, WV_OFF)
    outTbf = at_fixed("outTbf", [128, 8 * 512], BF16, OT_OFF + 16384)
    h1T = at_fixed("h1T", [128, 8 * 512], BF16, OT_OFF + 16384 + 8192)
    Ptw = at_fixed("Ptw", [128, 2048], F32, WK_OFF)
    C4N = at_fixed("C4N", [128, 768], F32, WQ_OFF + 10240)
    FFN_END = OT_OFF + 16384 + 8192 + 8192
    assert FFN_END <= AO_END, (FFN_END, AO_END)
    gcrow = at_fixed("gcrow", [1, 1024], BF16, OT_OFF)
    growsf = at_fixed("growsf", [1, 1024], F32, OT_OFF + 2048)
    g1pre = at_fixed("g1pre", [1, 512], F32, OT_OFF + 6144)
    g1b = at_fixed("g1b", [1, 512], BF16, OT_OFF + 8192)
    g2pre = at_fixed("g2pre", [1, 256], F32, OT_OFF + 9216)
    g2b = at_fixed("g2b", [1, 256], BF16, OT_OFF + 10240)

    LIMIT = TOP - BASE
    assert max(AO_END, FFN_END) <= LIMIT, (AO_END, FFN_END, LIMIT)

    pBIG = es.enter_context(nc.psum_tensor("pBIG", [128, 2048], F32))
    pQK = es.enter_context(nc.psum_tensor("pQK", [128, 1024], F32))
    pR = es.enter_context(nc.psum_tensor("pR", [128, 512], F32))
    pA = es.enter_context(nc.psum_tensor("pA", [128, 512], F32))
    pS = pBIG
    pX = pBIG[:, 1024:2048]
    SLOT = [pBIG[:, 0:512], pBIG[:, 512:1024], pBIG[:, 1024:1536], pBIG[:, 1536:2048]]

    sems = {}
    for name in ("s_winit s_const s_x0tok s_x0T s_x0f s_peq s_qkv s_prod s_sacc "
                 "s_exp2 s_rsum s_rcp2 s_rexp s_pnorm s_pexp s_pev s_prd s_pacc "
                 "s_aoev s_fc s_sum s_st s_rstd s_x1 s_gcs s_gc s_shad s_l s_h "
                 "s_tw s_x2 s_od s_odB s_x0tB s_x0fB s_dmp s_wq s_wf2 s_wgw s_wk s_wv s_shB s_krc s_ex4").split():
        sems[name] = Sem(nc, es, name)
    (s_winit, s_const, s_x0tok, s_x0T, s_x0f, s_peq, s_qkv, s_prod, s_sacc,
     s_exp2, s_rsum, s_rcp2, s_rexp, s_pnorm, s_pexp, s_pev, s_prd, s_pacc,
     s_aoev, s_fc, s_sum, s_st, s_rstd, s_x1, s_gcs, s_gc, s_shad, s_l, s_h,
     s_tw, s_x2, s_od, s_odB, s_x0tB, s_x0fB, s_dmp, s_wq, s_wf2, s_wgw,
     s_wk, s_wv, s_shB, s_krc, s_ex4) = (sems[n] for n in (
         "s_winit s_const s_x0tok s_x0T s_x0f s_peq s_qkv s_prod s_sacc s_exp2 "
         "s_rsum s_rcp2 s_rexp s_pnorm s_pexp s_pev s_prd s_pacc s_aoev s_fc "
         "s_sum s_st s_rstd s_x1 s_gcs s_gc s_shad s_l s_h s_tw s_x2 s_od "
         "s_odB s_x0tB s_x0fB s_dmp s_wq s_wf2 s_wgw s_wk s_wv s_shB s_krc s_ex4").split())
    x0tok_sems = [s_x0tok, s_x0tB]
    shad_sems = [None, None]
    x0f_sems = [s_x0f, s_x0fB]
    od_sems = [s_od, s_odB]

    class Prog:
        def __init__(self):
            self.pe = []
            self.act = []
            self.dve = []
            self.sp = []
            self.gp = []

    blocks = []

    def new_block():
        p = Prog()
        blocks.append(p)
        return p

    P = new_block()

    def gp_init(g):
        g.dma_start(cw1sb[:, :], bass.AP(dcw1, 0, [[1024, 128], [1024 * 128, 8], [128, 8], [1, 128]])).then_inc(s_winit.h, 16)
        g.dma_start(cw2sb[:, :], bass.AP(dcw2, 0, [[512, 128], [512 * 128, 8], [128, 4], [1, 128]])).then_inc(s_winit.h, 16)
        g.dma_start(cw3sb[:, :], bass.AP(dcw3, 0, [[256, 128], [256 * 128, 4], [128, 2], [1, 128]])).then_inc(s_winit.h, 16)
        g.dma_start(cw4sb[:, :], bass.AP(dcw4, 0, [[1, 128], [128, 2], [1, 1]]),
                    allow_slow_non_contiguous=True).then_inc(s_winit.h, 16)
        g.dma_start(cb1sb[:, :], bass.AP(dcb1, 0, [[1, 128], [128, 8]]),
                    allow_slow_non_contiguous=True).then_inc(s_winit.h, 16)
        g.dma_start(cb2sb[:, :], bass.AP(dcb2, 0, [[1, 128], [128, 4]]),
                    allow_slow_non_contiguous=True).then_inc(s_winit.h, 16)
        g.dma_start(cb3sb[:, :], bass.AP(dcb3, 0, [[1, 128], [128, 2]]),
                    allow_slow_non_contiguous=True).then_inc(s_winit.h, 16)
        g.dma_start(gb1row[0:1, :], bass.AP(dgb1, 0, [[512, 1], [1, 512]])).then_inc(s_winit.h, 16)
        g.dma_start(gb2row[0:1, :], bass.AP(dgb2, 0, [[256, 1], [1, 256]])).then_inc(s_winit.h, 16)
        g.dma_start(gw3sb[:, :], bass.AP(dgw3, 0, [[1, 128], [128, 2], [1, 1]]),
                    allow_slow_non_contiguous=True).then_inc(s_winit.h, 16)
        g.dma_start(arow[0:1, 0:1], bass.AP(dar, 0, [[1, 1], [1, 1]]),
                    allow_slow_non_contiguous=True).then_inc(s_winit.h, 16)
        g.dma_start(arow[0:1, 1:2], bass.AP(dcb4, 0, [[1, 1], [1, 1]]),
                    allow_slow_non_contiguous=True).then_inc(s_winit.h, 16)
        g.dma_start(arow[0:1, 2:3], bass.AP(dgb3, 0, [[1, 1], [1, 1]]),
                    allow_slow_non_contiguous=True).then_inc(s_winit.h, 16)
        g.dma_start(expf4_0[:, :], dex4.ap()[0:128, :]).then_inc(s_ex4.h, 16)
        g.dma_start(expf4_3[:, :], dex4.ap()[384:512, :]).then_inc(s_ex4.h, 16)
        g.iota(iwork[:, :], [[1, 128]], channel_multiplier=0, allow_small_or_imprecise_dtypes=True)
        g.iota(rcol[:, 0:1], [[0, 1]], channel_multiplier=1, allow_small_or_imprecise_dtypes=True)
        nc.gpsimd.engine_nop().then_inc(s_const.h, 1)
    P.gp.append(gp_init)
    s_ex4.inc(32)
    s_winit.inc(16 * 13)
    N_WINIT = s_winit.n
    s_const.inc()

    def ve_init(v):
        v.wait_ge(s_const.h, 1)
        v.memset(onesb[:, :], 1.0)
        v.memset(onesf[:, :], 1.0)
        v.memset(expb[:, :], EXP_BIAS)
        v.memset(epscol[:, :], LN_EPS)
        v.memset(onec[:, :], 1.0)
        v.tensor_scalar(rcol[:, 1:2], rcol[:, 0:1], 64.0, None, ALU.is_ge)
        v.drain()
        for t in range(16):
            v.tensor_scalar(iwork2[:, 0:1], rcol[:, 1:2], 16.0, float(t), ALU.mult, ALU.add)
            v.drain()
            v.tensor_scalar(iwork2[:, 32:64], iwork[:, 0:32], iwork2[:, 0:1], None, ALU.is_equal)
            v.drain()
            v.tensor_copy(indt[:, t * 32: t * 32 + 32], iwork2[:, 32:64])
            v.drain()
        v.tensor_scalar(iwork2[:, 0:1], rcol[:, 1:2], -64.0, rcol[:, 0:1], ALU.mult, ALU.add)
        v.drain()
        v.tensor_scalar(iwork2[:, 1:65], iwork[:, 0:64], iwork2[:, 0:1], None, ALU.is_equal)
        v.drain()
        v.memset(redda[:, :], 0.0)
        v.memset(reddb[:, :], 0.0)
        v.drain()
        v.tensor_copy(redda[:, 0:64], iwork2[:, 1:65])
        v.tensor_copy(reddb[:, 64:128], iwork2[:, 1:65])
        v.drain()
        v.tensor_scalar(iwork2[:, 0:1], rcol[:, 0:1], 1.0 / 16.0, None, ALU.mult)
        v.drain()
        v.tensor_scalar(iwork2[:, 8:16], iwork[:, 0:8], -1.0, iwork2[:, 0:1], ALU.mult, ALU.add)
        v.drain()
        v.tensor_scalar(iwork2[:, 16:24], iwork2[:, 8:16], 0.0, None, ALU.is_ge)
        v.tensor_scalar(iwork2[:, 24:32], iwork2[:, 8:16], 1.0, None, ALU.is_lt)
        v.drain()
        v.tensor_tensor(iwork2[:, 32:40], iwork2[:, 16:24], iwork2[:, 24:32], ALU.mult)
        v.drain()
        v.tensor_copy(rind[:, :], iwork2[:, 32:40])
        v.drain()
        v.memset(e8[:, :], 0.0)
        v.memset(rcpf1[:, :], 0.0)
        v.memset(rcpf2[:, :], 0.0)
        v.drain()
        v.tensor_scalar(iwork2[:, 0:128], iwork[:, 0:128], 1.0 / 16.0, None, ALU.mult)
        v.drain()
        v.tensor_scalar(iwork2[0:8, 0:128], iwork2[0:8, 0:128], rcol[0:8, 0:1], None, ALU.subtract)
        v.drain()
        v.tensor_scalar(iwork[0:8, 0:128], iwork2[0:8, 0:128], 0.0, None, ALU.is_ge)
        v.tensor_scalar(iwork2[0:8, 0:128], iwork2[0:8, 0:128], 1.0, None, ALU.is_lt)
        v.drain()
        v.tensor_tensor(e8[0:8, :], iwork[0:8, 0:128], iwork2[0:8, 0:128], ALU.mult)
        v.drain()
        nc.vector.engine_nop().then_inc(s_const.h, 1)
    P.dve.append(ve_init)
    s_const.inc()

    def gp_init2(g):
        g.wait_ge(s_const.h, 2)
        g.iota(iwork[:, :], [[1, 128]], channel_multiplier=0, allow_small_or_imprecise_dtypes=True)
        nc.gpsimd.engine_nop().then_inc(s_const.h, 1)
    P.gp.append(gp_init2)
    s_const.inc()

    def ve_init2(v):
        v.wait_ge(s_const.h, 3)
        nc.vector.engine_nop().then_inc(s_const.h, 1)
    P.dve.append(ve_init2)
    s_const.inc()

    def gp_init3(g):
        g.wait_ge(s_const.h, 4)
        g.iota(iwork[:, :], [[1, 128]], channel_multiplier=0, allow_small_or_imprecise_dtypes=True)
        nc.gpsimd.engine_nop().then_inc(s_const.h, 1)
    P.gp.append(gp_init3)
    s_const.inc()

    def ve_init3(v):
        v.wait_ge(s_const.h, 5)
        v.tensor_scalar(idf32[:, :], iwork[:, 0:128], rcol[:, 0:1], None, ALU.is_equal)
        v.drain()
        nc.vector.engine_nop().then_inc(s_const.h, 1)
    P.dve.append(ve_init3)
    CONST_DONE = s_const.inc()

    x2_free = [0, 0]
    KRC = [0]
    OBG = [0, 0]
    shad_sems[0] = s_shad
    shad_sems[1] = s_shB

    for b in range(B_LOC):
        P = new_block()

        wqkv_base = s_wq.n

        def gp_w(g, b=b, odA=s_od.n, odB=s_odB.n):
            if odA > 0:
                g.wait_ge(s_od.h, odA)
            if odB > 0:
                g.wait_ge(s_odB.h, odB)
            for dst, src, sem in ((wqsb, dwq, s_wq), (wksb, dwk, s_wk), (wvsb, dwv, s_wv)):
                g.dma_start(dst[:, :], bass.AP(src, 0, [[1024, 128], [1024 * 128, 8], [128, 8], [1, 128]])).then_inc(sem.h, 16)
            g.dma_start(expf4_1[:, :], dex4.ap()[128:256, :]).then_inc(s_ex4.h, 16)
            g.dma_start(expf4_2[:, :], dex4.ap()[256:384, :]).then_inc(s_ex4.h, 16)
        gp_w_pending = [gp_w]
        wq_tgts = [s_wq.inc(16), s_wk.inc(16), s_wv.inc(16)]
        wq_sems = [s_wq, s_wk, s_wv]
        ex4_tgt = s_ex4.inc(32)

        pend_pv = None
        qkv_bank_free = {0: 0, 1: 0}
        sacc_track = {0: 0, 1: 0}
        pvb_track = {0: 0, 1: 0, 2: 0}
        sl_prd = {0: 0, 1: 0}
        sl_pacc = {0: 0, 1: 0}
        ao_gate_v = 0
        exp_gate_prev = 0
        pnorm_gate_prev = 0
        aoev_prev = 0

        for sb_ in range(NSB):
            par = sb_ % 2
            for tt4 in range(4):
                tt = sb_ * 4 + tt4

                tsem = x0tok_sems[tt4 % 2]
                gate = s_x0T.n

                def gp2(g, b=b, tt=tt, tt4=tt4, gate=gate, tsem=tsem,
                        first=(sb_ == 0 and tt4 == 0 and b == 0)):
                    if first:
                        g.wait_ge(s_const.h, CONST_DONE)
                    if gate > 0:
                        g.wait_ge(s_x0T.h, gate)
                    g.dma_start(x0tok[:, (tt4 % 2) * 1024: (tt4 % 2) * 1024 + 1024],
                                dx.ap()[b * S + tt * 128: b * S + (tt + 1) * 128, :]).then_inc(tsem.h, 16)
                P.gp.append(gp2)
                if gp_w_pending and sb_ == 0 and tt4 == 1:
                    P.gp.append(gp_w_pending.pop())
                tok_t = tsem.inc(16)

                x0T_reuse = 0 if sb_ < 2 else peq_after_sb[sb_ - 2]

                def sp(s, tt4=tt4, tok_t=tok_t, par=par, x0T_reuse=x0T_reuse,
                       tsem=tsem):
                    s.wait_ge(tsem.h, tok_t)
                    if x0T_reuse > 0:
                        s.wait_ge(s_peq.h, x0T_reuse)
                    for c in range(8):
                        s.dma_start(x0T[:, par * 4096 + c * 512 + tt4 * 128: par * 4096 + c * 512 + tt4 * 128 + 128],
                                    x0tok[:, (tt4 % 2) * 1024 + c * 128: (tt4 % 2) * 1024 + c * 128 + 128],
                                    transpose=True).then_inc(s_x0T.h, 16)
                P.sp.append(sp)
                s_x0T.inc(16 * 8)
            x0T_tgt = s_x0T.n
            if sb_ == 0:
                peq_after_sb = {}

            qk_prev_consumed = 0 if sb_ == 0 else sacc_final
            v_prev_consumed = 0 if sb_ == 0 else prd_final
            QPE, QAC, VPE, VAC = [], [], [], []
            qkv_tgt_qk = 0
            for mi in range(3):
                wsb = (wqsb, wksb, wvsb)[mi]
                dst = (qtsb, ktsb, vtsb)[mi]
                ev_prev = qk_prev_consumed if mi < 2 else v_prev_consumed
                ev_sem = s_sacc if mi < 2 else s_prd
                for dch in range(8):
                    bank = pQK[:, (dch % 2) * 512: (dch % 2) * 512 + 512]
                    ev_gate = qkv_bank_free[dch % 2]

                    def pe(te, wsb=wsb, dch=dch, bank=bank, par=par, x0T_tgt=x0T_tgt,
                           first=(mi == 0 and dch == 0), firstm=(dch == 0), ev_gate=ev_gate,
                           wsem=wq_sems[mi], wq_t=wq_tgts[mi], sb_=sb_, pv2g=pvb_track[2]):
                        if first:
                            te.wait_ge(s_x0T.h, x0T_tgt)
                            if pv2g > 0:
                                te.wait_ge(s_pev.h, pv2g)
                        if firstm and sb_ == 0:
                            te.wait_ge(wsem.h, wq_t)
                        if ev_gate > 0:
                            te.wait_ge(s_qkv.h, ev_gate)
                        for k in range(8):
                            mm = te.matmul(bank,
                                           wsb[:, (k * 8 + dch) * 128: (k * 8 + dch) * 128 + 128],
                                           x0T[:, par * 4096 + k * 512: par * 4096 + k * 512 + 512],
                                           start=(k == 0), stop=(k == 7))
                        mm.then_inc(s_peq.h, 1)
                    (QPE if mi < 2 else VPE).append(pe)
                    peq_t = s_peq.inc()

                    def ac(a, peq_t=peq_t, dst=dst, dch=dch, bank=bank,
                           ev_prev=ev_prev, ev_sem=ev_sem, first=(dch == 0)):
                        a.wait_ge(s_peq.h, peq_t)
                        if first and ev_prev > 0:
                            a.wait_ge(ev_sem.h, ev_prev)
                        a.activation(dst[:, dch * 512: dch * 512 + 512], bank,
                                     AF.Copy).then_inc(s_qkv.h, 1)
                    (QAC if mi < 2 else VAC).append(ac)
                    qkv_bank_free[dch % 2] = s_qkv.inc()
                if mi == 1:
                    qkv_tgt_qk = s_qkv.n
            peq_after_sb[sb_] = s_peq.n
            qkv_tgt = s_qkv.n

            if pend_pv is not None:
                for kind, f in pend_pv[0]:
                    P.pe.append(f)
                for tag, f in pend_pv[1]:
                    P.act.append(f)
                pend_pv = None
            for f in QPE:
                P.pe.append(f)
            for f in QAC:
                P.act.append(f)
            for f in VPE:
                P.pe.append(f)
            for f in VAC:
                P.act.append(f)

            prod_hist = {}
            for t in range(16):
                krbuf = t % 2
                ga = sacc_track[0]
                gb = sacc_track[1]
                kr_cov = prod_hist.get(t - 2, 0)

                def gpK(g, t=t, krbuf=krbuf, qkv_tgt=qkv_tgt_qk,
                        first=(t == 0), kr_cov=kr_cov, kprev=s_krc.n):
                    if first:
                        g.wait_ge(s_qkv.h, qkv_tgt)
                    if kprev > 0:
                        g.wait_ge(s_krc.h, kprev)
                    if kr_cov > 0:
                        g.wait_ge(s_prod.h, kr_cov)
                    src0 = (t % 2) * 64
                    ch = t // 2
                    g.dma_start(kr[0:64, krbuf * 512: krbuf * 512 + 512],
                                ktsb[src0:src0 + 64, ch * 512: ch * 512 + 512]).then_inc(s_krc.h, 16)
                    g.dma_start(kr[64:128, krbuf * 512: krbuf * 512 + 512],
                                ktsb[src0:src0 + 64, ch * 512: ch * 512 + 512]).then_inc(s_krc.h, 16)
                P.gp.append(gpK)
                krc_t = s_krc.inc(32)

                def dvK(v, t=t, krbuf=krbuf, krc_t=krc_t, ga=ga, gb=gb):
                    v.wait_ge(s_krc.h, krc_t)
                    krep = bass.AP(kr, krbuf * 512, [[1024, 128], [0, 4], [1, 512]])
                    if ga > 0:
                        v.wait_ge(s_sacc.h, ga)
                    v.tensor_tensor(
                        bass.AP(prodb, 0, [[4096, 128], [512, 4], [1, 512]]),
                        bass.AP(qtsb, 0, [[4096, 128], [512, 4], [1, 512]]),
                        krep, ALU.mult).then_inc(s_prod.h, 1)
                    if gb > 0:
                        v.wait_ge(s_sacc.h, gb)
                    v.tensor_tensor(
                        bass.AP(prodb, 2048, [[4096, 128], [512, 4], [1, 512]]),
                        bass.AP(qtsb, 2048, [[4096, 128], [512, 4], [1, 512]]),
                        krep, ALU.mult).then_inc(s_prod.h, 1)
                P.dve.append(dvK)
                prod_a = s_prod.inc()
                prod_b = s_prod.inc()
                prod_hist[t] = prod_b

                for c in range(8):
                    prod_t = prod_a if c < 4 else prod_b

                    def peS(te, t=t, c=c, prod_t=prod_t,
                            first=(t == 0 and c == 0), sgg=exp_gate_prev,
                            pvg=(0 if sb_ == 0 else pev_final_prev)):
                        te.wait_ge(s_prod.h, prod_t)
                        if first and sgg > 0:
                            te.wait_ge(s_exp2.h, sgg)
                        if first and pvg > 0:
                            te.wait_ge(s_pev.h, pvg)
                        mm = te.matmul(pS[32 * (c % 4): 32 * (c % 4) + 32,
                                          (c // 4) * 512: (c // 4) * 512 + 512],
                                       indt[:, t * 32: t * 32 + 32],
                                       prodb[:, c * 512: c * 512 + 512],
                                       start=(t == 0), stop=(t == 15),
                                       tile_position=(0, 32 * (c % 4)),
                                       skip_group_check=True)
                        mm.then_inc(s_sacc.h, 1)
                    P.pe.append(peS)
                    if c == 3:
                        sacc_track[0] = s_sacc.inc()
                    elif c == 7:
                        sacc_track[1] = s_sacc.inc()
                    else:
                        s_sacc.inc()
            sacc_final = s_sacc.n

            def acE(a, sacc_final=sacc_final, png=pnorm_gate_prev):
                a.wait_ge(s_sacc.h, sacc_final)
                if png > 0:
                    a.wait_ge(s_pnorm.h, png)
                a.activation(praw[:, 0:1024], pS[:, 0:1024], AF.Exp,
                             bias=expb[:, :], scale=EXP_SCALE).then_inc(s_exp2.h, 1)
            P.act.append(acE)
            exp_t = s_exp2.inc()
            exp_gate_prev = exp_t

            pfm_reuse = 0 if sb_ == 0 else pexp_final

            def peR(te, exp_t=exp_t, rg=(0 if sb_ == 0 else s_rcp2.n)):
                te.wait_ge(s_exp2.h, exp_t)
                if rg > 0:
                    te.wait_ge(s_rcp2.h, rg)
                te.matmul(pR[0:8, 0:512], rind[:, :], praw[:, 0:512], start=True, stop=True)
                te.matmul(pA[0:8, 0:512], rind[:, :], praw[:, 512:1024],
                          start=True, stop=True).then_inc(s_rsum.h, 1)
            P.pe.append(peR)
            rsum_t = s_rsum.inc()

            def acR(a, rsum_t=rsum_t):
                a.wait_ge(s_rsum.h, rsum_t)
                a.activation(lnb[:, :], pR[0:8, 0:512], AF.Ln)
                a.drain()
                a.activation(rcpf1[0:8, :], lnb[:, :], AF.Exp, scale=-1.0)
                a.drain()
                a.activation(lnb[:, :], pA[0:8, 0:512], AF.Ln)
                a.drain()
                a.activation(rcpf2[0:8, :], lnb[:, :], AF.Exp, scale=-1.0).then_inc(s_rcp2.h, 1)
            P.act.append(acR)
            rcp_t = s_rcp2.inc()

            def peX(te, rcp_t=rcp_t, pxg=pvb_track[1]):
                te.wait_ge(s_rcp2.h, rcp_t)
                if pxg > 0:
                    te.wait_ge(s_pev.h, pxg)
                te.matmul(pBIG[:, 1024:1536], e8[:, :], rcpf1[:, :], start=True, stop=True)
                te.matmul(pBIG[:, 1536:2048], e8[:, :], rcpf2[:, :],
                          start=True, stop=True).then_inc(s_rexp.h, 1)
            P.pe.append(peX)
            rexp_t = s_rexp.inc()

            def dvN(v, rexp_t=rexp_t, pfm_reuse=pfm_reuse):
                v.wait_ge(s_rexp.h, rexp_t)
                if pfm_reuse > 0:
                    v.wait_ge(s_pexp.h, pfm_reuse)
                v.tensor_tensor(pfm[:, 0:1024], pBIG[:, 1024:2048], praw[:, 0:1024],
                                ALU.mult).then_inc(s_pnorm.h, 1)
            P.dve.append(dvN)
            pnorm_t = s_pnorm.inc()
            pnorm_gate_prev = pnorm_t

            ECL = []
            RCL = []
            AVL = []
            ao_bank = {0: ao_gate_v, 1: ao_gate_v}
            for pi, (cp, k) in enumerate([(g * 2 + sub, kk)
                                          for g in range(4)
                                          for kk in range(8)
                                          for sub in range(2)]):
                pb = pi % 3
                slot = pi % 2
                grp = cp // 4
                exv = (expf4_0, expf4_1, expf4_2, expf4_3)[cp % 4]
                rbank = pA if cp % 2 == 0 else pR
                pvb = (pBIG[:, 0:1024], pBIG[:, 1024:2048], pQK[:, 0:1024])[pb]
                bank_gate = pvb_track[pb]
                qg = qkv_tgt if pi == 2 else 0

                def peE(te, pvb=pvb, exv=exv, grp=grp, k=k, pnorm_t=pnorm_t,
                        first=(pi == 0), bank_gate=bank_gate, qg=qg, e4t=ex4_tgt):
                    if first:
                        te.wait_ge(s_pnorm.h, pnorm_t)
                        te.wait_ge(s_ex4.h, e4t)
                    if qg > 0:
                        te.wait_ge(s_qkv.h, qg)
                    if bank_gate > 0:
                        te.wait_ge(s_pev.h, bank_gate)
                    for hh in range(2):
                        mm = te.matmul(pvb[:, hh * 512: hh * 512 + 512],
                                       exv[:, (hh * 8 + k) * 128: (hh * 8 + k) * 128 + 128],
                                       pfm[:, grp * 512: grp * 512 + 512],
                                       start=True, stop=True)
                    mm.then_inc(s_pexp.h, 1)
                ECL.append(peE)
                pexp_t = s_pexp.inc()

                ev_gate = sl_prd[slot]

                def acV(a, pvb=pvb, slot=slot, pexp_t=pexp_t, ev_gate=ev_gate):
                    a.wait_ge(s_pexp.h, pexp_t)
                    if ev_gate > 0:
                        a.wait_ge(s_prd.h, ev_gate)
                    a.activation(pexpsb[:, slot * 1024: slot * 1024 + 1024],
                                 pvb, AF.Copy).then_inc(s_pev.h, 1)
                AVL.append((pi, acV))
                pvb_track[pb] = s_pev.inc()

                prd_gate = sl_pacc[slot]
                pev_t = pvb_track[pb]

                def dvD(v, slot=slot, k=k, pev_t=pev_t, prd_gate=prd_gate,
                        vq=(qkv_tgt if pi == 0 else 0)):
                    v.wait_ge(s_pev.h, pev_t)
                    if vq > 0:
                        v.wait_ge(s_qkv.h, vq)
                    if prd_gate > 0:
                        v.wait_ge(s_pacc.h, prd_gate)
                    v.tensor_tensor(
                        bass.AP(prdsb, slot * 1024, [[2048, 128], [512, 2], [1, 512]]),
                        bass.AP(pexpsb, slot * 1024, [[2048, 128], [512, 2], [1, 512]]),
                        bass.AP(vtsb, k * 512, [[4096, 128], [0, 2], [1, 512]]),
                        ALU.mult).then_inc(s_prd.h, 1)
                P.dve.append(dvD)
                sl_prd[slot] = s_prd.inc()

                prd_t = sl_prd[slot]
                ag = ao_bank[cp % 2]

                def peA(te, slot=slot, k=k, rbank=rbank, prd_t=prd_t, ag=ag):
                    te.wait_ge(s_prd.h, prd_t)
                    if k == 0 and ag > 0:
                        te.wait_ge(s_aoev.h, ag)
                    for hh in range(2):
                        te.matmul(rbank[hh * 64: hh * 64 + 64, :],
                                  redda[:, 0:64],
                                  prdsb[:, slot * 1024 + hh * 512: slot * 1024 + hh * 512 + 512],
                                  start=(k == 0), stop=(k == 7),
                                  tile_position=(0, hh * 64),
                                  skip_group_check=True).then_inc(s_pacc.h, 1)
                RCL.append(peA)
                s_pacc.inc()
                sl_pacc[slot] = s_pacc.inc()

                if k == 7:
                    pacc_fin = s_pacc.n

                    def acA(a, cp=cp, sb_=sb_, rbank=rbank, pacc_fin=pacc_fin):
                        a.wait_ge(s_pacc.h, pacc_fin)
                        a.activation(attnoutT[:, cp * 2048 + sb_ * 512: cp * 2048 + sb_ * 512 + 512],
                                     rbank[:, :], AF.Copy).then_inc(s_aoev.h, 1)
                    AVL.append((pi, acA))
                    ao_bank[cp % 2] = s_aoev.inc()
            ao_gate_v = max(ao_bank.values())
            DEPTH = 3
            pe_pipe = []
            for p in range(64 + DEPTH):
                if p < 64:
                    pe_pipe.append(('E', ECL[p]))
                if p >= DEPTH:
                    pe_pipe.append(('R', RCL[p - DEPTH]))
            pend_pv = (pe_pipe, AVL)
            pexp_final = s_pexp.n
            prd_final = s_prd.n
            pev_final_prev = s_pev.n
            aoev_prev = s_aoev.n
        if pend_pv is not None:
            for kind, f in pend_pv[0]:
                P.pe.append(f)
            for tag, f in pend_pv[1]:
                P.act.append(f)
            pend_pv = None
        ao_done = s_aoev.n

        def gp_wfc(g, pf=peq_after_sb[NSB - 1]):
            g.wait_ge(s_peq.h, pf)
            g.dma_start(wfcsb[:, :], bass.AP(dwfc, 0, [[1024, 128], [1024 * 128, 8], [1, 1024]])).then_inc(s_wf2.h, 16)
        P.gp.append(gp_wfc)
        wfc_tgt = s_wf2.inc(16)

        fc_bank_free = {0: 0, 1: 0, 2: 0, 3: 0}
        x1_final = 0
        gcs_t = 0
        x0f_gate_prev = 0
        peq_final_b = s_peq.n
        pend_dv4 = [None]

        def emit_dv4(gcg=0, P=P):
            fn, tt_p = pend_dv4[0]
            if gcg > 0:
                def wrapped(v, fn=fn, gcg=gcg):
                    v.wait_ge(s_gcs.h, gcg)
                    fn(v)
                P.dve.append(wrapped)
            else:
                P.dve.append(fn)
            x1_t = s_x1.inc()

            def pe4(te, x1_t=x1_t, tt=tt_p, rcp_gate=s_rcp2.n):
                te.wait_ge(s_x1.h, x1_t)
                if tt == 0:
                    te.wait_ge(s_rcp2.h, rcp_gate)
                m1 = te.matmul(pR[0:1, 0:512], onesb[:, 0:1], x1bfgc[:, 0:512],
                               start=(tt == 0), stop=(tt == NTT - 1))
                m2 = te.matmul(pA[0:1, 0:512], onesb[:, 0:1], x1bfgc[:, 512:1024],
                               start=(tt == 0), stop=(tt == NTT - 1))
                m2.then_inc(s_gcs.h, 1)
            P.pe.append(pe4)
            s_gcs.inc()
            pend_dv4[0] = None
            return x1_t

        for tt in range(NTT):
            par = tt % 2
            stile = sumtile if par == 0 else sumtile2
            gate = 0 if tt < 2 else x0f_gate_prev
            fsem = x0f_sems[tt % 2]

            def sp2(s, tt=tt, b=b, gate=gate, fsem=fsem, pqf=peq_final_b):
                if tt < 2:
                    s.wait_ge(s_peq.h, pqf)
                if gate > 0:
                    s.wait_ge(s_sum.h, gate)
                s.dma_start(x0f32[:, (tt % 2) * 1024: (tt % 2) * 1024 + 1024],
                            dx.ap()[b * S + tt * 128: b * S + (tt + 1) * 128, :]).then_inc(fsem.h, 16)
            P.sp.append(sp2)
            x0f_t = fsem.inc(16)

            slot0 = (tt % 2) * 2
            sum_t = 0
            for half in range(2):
                slot = slot0 + half
                bfree = fc_bank_free[slot]

                def pe3(te, tt=tt, half=half, slot=slot, ao_done=ao_done, bfree=bfree,
                        first=(tt == 0 and half == 0), wfc_tgt=wfc_tgt):
                    if first:
                        te.wait_ge(s_aoev.h, ao_done)
                        te.wait_ge(s_wf2.h, wfc_tgt)
                        te.wait_ge(s_winit.h, N_WINIT)
                    if bfree > 0:
                        te.wait_ge(s_sum.h, bfree)
                    for c in range(8):
                        mm = te.matmul(SLOT[slot],
                                       attnoutT[:, c * 2048 + tt * 128: c * 2048 + tt * 128 + 128],
                                       wfcsb[:, c * 1024 + half * 512: c * 1024 + half * 512 + 512],
                                       start=(c == 0), stop=(c == 7))
                    mm.then_inc(s_fc.h, 1)
                P.pe.append(pe3)
                fc_t = s_fc.inc()

                def dv2(v, fc_t=fc_t, x0f_t=x0f_t, half=half, tt=tt, slot=slot,
                        fsem=fsem, stile=stile):
                    v.wait_ge(s_fc.h, fc_t)
                    v.wait_ge(fsem.h, x0f_t)
                    v.tensor_tensor(stile[:, half * 512: half * 512 + 512],
                                    SLOT[slot],
                                    x0f32[:, (tt % 2) * 1024 + half * 512: (tt % 2) * 1024 + half * 512 + 512],
                                    ALU.add).then_inc(s_sum.h, 1)
                P.dve.append(dv2)
                sum_t = s_sum.inc()
                fc_bank_free[slot] = sum_t
            x0f_gate_prev = sum_t

            def dv3(v, sum_t=sum_t, par=par, stile=stile):
                v.wait_ge(s_sum.h, sum_t)
                v.drain()
                v.bn_stats(stats[:, 0:6], stile[:, 0:512])
                v.bn_stats(stats[:, 6:12], stile[:, 512:1024])
                v.drain()
                v.bn_aggr((aggr if par == 0 else aggr2)[:, :], stats[:, :])
                v.drain()
                nc.vector.engine_nop().then_inc(s_st.h, 1)
            P.dve.append(dv3)
            st_t = s_st.inc()

            def ac3(a, st_t=st_t, par=par):
                a.wait_ge(s_st.h, st_t)
                a.activation(lncol[:, par * 2: par * 2 + 1],
                             (aggr if par == 0 else aggr2)[:, 1:2],
                             AF.Ln, bias=epscol[:, :])
                a.drain()
                a.activation(lncol[:, par * 2 + 1: par * 2 + 2], lncol[:, par * 2: par * 2 + 1],
                             AF.Exp, scale=-0.5).then_inc(s_rstd.h, 1)
            P.act.append(ac3)
            rstd_t = s_rstd.inc()

            if pend_dv4[0] is not None:
                x1_final = emit_dv4()

            def dv4(v, rstd_t=rstd_t, tt=tt, par=par, stile=stile):
                v.wait_ge(s_rstd.h, rstd_t)
                ag = aggr if par == 0 else aggr2
                v.tensor_scalar(x1[:, tt * 1024: tt * 1024 + 1024], stile[:, :],
                                ag[:, 0:1], lncol[:, par * 2 + 1: par * 2 + 2],
                                ALU.subtract, ALU.mult)
                v.tensor_scalar(x1bfgc[:, :], stile[:, :],
                                ag[:, 0:1], lncol[:, par * 2 + 1: par * 2 + 2],
                                ALU.subtract, ALU.mult)
                v.drain()
                nc.vector.engine_nop().then_inc(s_x1.h, 1)
            pend_dv4[0] = (dv4, tt)
        x1_final = emit_dv4(gcg=s_gcs.n)
        gcs_t = s_gcs.n

        def gp_gw(g, x1_f=x1_final, gcs_f=s_gcs.n):
            g.wait_ge(s_x1.h, x1_f)
            g.wait_ge(s_gcs.h, gcs_f)
            g.dma_start(gw1sb[:, :], bass.AP(dgw1, 0, [[512, 128], [512 * 128, 8], [1, 512]])).then_inc(s_wgw.h, 16)
            g.dma_start(gw2sb[:, :], bass.AP(dgw2, 0, [[256, 128], [256 * 128, 4], [1, 256]])).then_inc(s_wgw.h, 16)
        P.gp.append(gp_gw)
        gw_tgt = s_wgw.inc(16 * 2)

        if b == 0 and DEBUG[0]:
            def sp_dbg(s, x1f=x1_final, aof=ao_done):
                s.wait_ge(s_x1.h, x1f)
                s.wait_ge(s_aoev.h, aof)
                for tt in range(NTT):
                    s.dma_start(dbg_x1.ap()[tt * 128:(tt + 1) * 128, :],
                                x1[:, tt * 1024: tt * 1024 + 1024]).then_inc(s_dmp.h, 16)
                s.dma_start(dbg_ao.ap()[:, :], attnoutT[:, :]).then_inc(s_dmp.h, 16)
            P.sp.append(sp_dbg)
            dump_tgt = s_dmp.inc(16 * 17)
        else:
            dump_tgt = 0

        fc_final_c = s_fc.n
        l_prev_seq = s_l.n
        sum_final_c = s_sum.n
        l_bank_free = {0: 0, 1: 0, 2: 0, 3: 0}
        l2_bank_free = {0: 0, 1: 0}
        h1_gate = {0: 0, 1: 0}
        h2_gate = {0: 0, 1: 0}
        h3_gate = {0: 0, 1: 0}
        shad_ready_h = {0: 0, 1: 0}
        shad_base_h = {0: 0, 1: 0}
        cast_ready_h = {0: 0, 1: 0}
        l1_after = {0: 0, 1: 0}
        tf_free = [0]
        epi_free = [0]
        c4n_free = [0]
        outT_free = [0, 0]
        stile_free = {0: 0, 1: 0}
        OBGG = [None]
        dmp272 = 272 if (b == 0 and DEBUG[0]) else 0

        def emit_init(T, hf):
            shad_base_h[hf] = shad_sems[hf].n
            for sti in range(2):
                st = 2 * hf + sti
                og = OBGG[0]
                lg = l_prev_seq if T == 0 else l1_after[hf]

                def dvI(v, T=T, st=st, og=og, x1f=x1_final,
                        wqt=(wq_tgts if T == 0 else None)):
                    v.wait_ge(s_x1.h, x1f)
                    if wqt is not None:
                        v.wait_ge(s_wq.h, wqt[0])
                        v.wait_ge(s_wk.h, wqt[1])
                        v.wait_ge(s_wv.h, wqt[2])
                    if og is not None:
                        v.wait_ge(og[0].h, og[1])
                    v.tensor_copy(xstg[:, :],
                                  x1[:, (T * 4 + st) * 1024: (T * 4 + st) * 1024 + 1024])
                    v.drain()
                    nc.vector.engine_nop().then_inc(s_tw.h, 1)
                P.dve.append(dvI)
                ct = s_tw.inc()

                def spI(s, st=st, ct=ct, lg=lg, hsem=shad_sems[hf],
                        fcf=(fc_final_c if T == 0 else 0),
                        dmp=(dmp272 if T == 0 else 0)):
                    s.wait_ge(s_tw.h, ct)
                    if fcf > 0:
                        s.wait_ge(s_fc.h, fcf)
                    if dmp > 0:
                        s.wait_ge(s_dmp.h, dmp)
                    if lg > 0:
                        s.wait_ge(s_l.h, lg)
                    for c in range(8):
                        s.dma_start(outTbf[:, c * 512 + st * 128: c * 512 + st * 128 + 128],
                                    xstg[:, c * 128: c * 128 + 128],
                                    transpose=True).then_inc(hsem.h, 16)
                P.sp.append(spI)
                OBGG[0] = (shad_sems[hf], shad_sems[hf].inc(16 * 8))
            shad_ready_h[hf] = shad_sems[hf].n

        def emit_widen(T, hf):
            par = T % 2
            oT = (outT0, outT1)[par]
            sr = shad_ready_h[hf]
            ofree = outT_free[par]

            def dvW(v, oT=oT, hf=hf, sr=sr, hsem=shad_sems[hf], ofree=ofree,
                    gcg=(gcnet_done if (T == 0 and hf == 0) else 0)):
                v.wait_ge(hsem.h, sr)
                if gcg > 0:
                    v.wait_ge(s_gc.h, gcg)
                if ofree > 0:
                    v.wait_ge(s_aoev.h, ofree)
                v.tensor_copy(
                    bass.AP(oT, hf * 256, [[4096, 128], [512, 8], [1, 256]]),
                    bass.AP(outTbf, hf * 256, [[4096, 128], [512, 8], [1, 256]]))
                v.drain()
                nc.vector.engine_nop().then_inc(s_tw.h, 1)
            P.dve.append(dvW)
            return s_tw.inc()

        for hf_ in range(2):
            emit_init(0, hf_)

        def emit_gc(fn, engine):
            getattr(P, engine).append(fn)
            return s_gc.inc()

        g_t = emit_gc(lambda v, gcs_t=gcs_t, gw_tgt=gw_tgt, dt_=dump_tgt: (
            v.wait_ge(s_gcs.h, gcs_t),
            v.wait_ge(s_wgw.h, gw_tgt),
            (v.wait_ge(s_dmp.h, dt_) if dt_ > 0 else None),
            v.tensor_copy(growsf[0:1, 0:512], pR[0:1, 0:512]),
            v.tensor_copy(growsf[0:1, 512:1024], pA[0:1, 0:512]),
            v.drain(),
            v.tensor_scalar(gcrow[0:1, :], growsf[0:1, :], 1.0 / S, None, ALU.mult),
            v.drain(),
            nc.vector.engine_nop().then_inc(s_gc.h, 1),
        ), "dve")

        def pe5(te, g_t=g_t):
            te.wait_ge(s_gc.h, g_t)
            for c in range(8):
                mm = te.matmul(pR[:, c:c + 1], gcrow[0:1, c * 128: c * 128 + 128],
                               onesb[0:1, 0:1], start=True, stop=True,
                               skip_group_check=True)
            mm.then_inc(s_gc.h, 1)
        g_t = emit_gc(pe5, "pe")

        g_t = emit_gc(lambda v, g_t=g_t: (
            v.wait_ge(s_gc.h, g_t),
            v.tensor_copy(gcT[:, :], pR[:, 0:8]),
            v.drain(),
            nc.vector.engine_nop().then_inc(s_gc.h, 1),
        ), "dve")

        def pe6(te, g_t=g_t):
            te.wait_ge(s_gc.h, g_t)
            for c in range(8):
                mm = te.matmul(pA[0:1, 0:512], gcT[:, c:c + 1],
                               gw1sb[:, c * 512: c * 512 + 512], start=(c == 0), stop=(c == 7),
                               skip_group_check=True)
            mm.then_inc(s_gc.h, 1)
        g_t = emit_gc(pe6, "pe")

        g_t = emit_gc(lambda v, g_t=g_t: (
            v.wait_ge(s_gc.h, g_t),
            v.tensor_tensor(g1pre[0:1, :], pA[0:1, 0:512], gb1row[0:1, :], ALU.add),
            v.drain(),
            nc.vector.engine_nop().then_inc(s_gc.h, 1),
        ), "dve")

        g_t = emit_gc(lambda a, g_t=g_t: (
            a.wait_ge(s_gc.h, g_t),
            a.activation(g1b[0:1, :], g1pre[0:1, :], AF.Gelu).then_inc(s_gc.h, 1),
        ), "act")

        def pe7(te, g_t=g_t):
            te.wait_ge(s_gc.h, g_t)
            for c in range(4):
                mm = te.matmul(pR[:, 16 + c: 17 + c], g1b[0:1, c * 128: c * 128 + 128],
                               onesb[0:1, 0:1], start=True, stop=True,
                               skip_group_check=True)
            mm.then_inc(s_gc.h, 1)
        g_t = emit_gc(pe7, "pe")

        g_t = emit_gc(lambda v, g_t=g_t: (
            v.wait_ge(s_gc.h, g_t),
            v.tensor_copy(g1T[:, :], pR[:, 16:20]),
            v.drain(),
            nc.vector.engine_nop().then_inc(s_gc.h, 1),
        ), "dve")

        def pe8(te, g_t=g_t):
            te.wait_ge(s_gc.h, g_t)
            for c in range(4):
                mm = te.matmul(pR[0:1, 256:512], g1T[:, c:c + 1],
                               gw2sb[:, c * 256: c * 256 + 256], start=(c == 0), stop=(c == 3),
                               skip_group_check=True)
            mm.then_inc(s_gc.h, 1)
        g_t = emit_gc(pe8, "pe")

        g_t = emit_gc(lambda v, g_t=g_t: (
            v.wait_ge(s_gc.h, g_t),
            v.tensor_tensor(g2pre[0:1, :], pR[0:1, 256:512], gb2row[0:1, :], ALU.add),
            v.drain(),
            nc.vector.engine_nop().then_inc(s_gc.h, 1),
        ), "dve")

        g_t = emit_gc(lambda a, g_t=g_t: (
            a.wait_ge(s_gc.h, g_t),
            a.activation(g2b[0:1, :], g2pre[0:1, :], AF.Gelu).then_inc(s_gc.h, 1),
        ), "act")

        def pe9(te, g_t=g_t):
            te.wait_ge(s_gc.h, g_t)
            for c in range(2):
                mm = te.matmul(pR[:, 24 + c: 25 + c], g2b[0:1, c * 128: c * 128 + 128],
                               onesb[0:1, 0:1], start=True, stop=True,
                               skip_group_check=True)
            mm.then_inc(s_gc.h, 1)
        g_t = emit_gc(pe9, "pe")

        g_t = emit_gc(lambda v, g_t=g_t: (
            v.wait_ge(s_gc.h, g_t),
            v.tensor_copy(g2T[:, :], pR[:, 24:26]),
            v.drain(),
            nc.vector.engine_nop().then_inc(s_gc.h, 1),
        ), "dve")

        def pe10(te, g_t=g_t):
            te.wait_ge(s_gc.h, g_t)
            for c in range(2):
                mm = te.matmul(pR[0:1, 30:31], g2T[:, c:c + 1], gw3sb[:, c:c + 1],
                               start=(c == 0), stop=(c == 1),
                               skip_group_check=True)
            mm.then_inc(s_gc.h, 1)
        g_t = emit_gc(pe10, "pe")

        g_t = emit_gc(lambda v, g_t=g_t: (
            v.wait_ge(s_gc.h, g_t),
            v.tensor_tensor(g3pre[0:1, :], pR[0:1, 30:31], arow[0:1, 2:3], ALU.add),
            v.drain(),
            nc.vector.engine_nop().then_inc(s_gc.h, 1),
        ), "dve")

        g_t = emit_gc(lambda a, g_t=g_t: (
            a.wait_ge(s_gc.h, g_t),
            a.activation(gtanh[0:1, :], g3pre[0:1, :], AF.Tanh, scale=0.5).then_inc(s_gc.h, 1),
        ), "act")

        def dv12(v, g_t=g_t):
            v.wait_ge(s_gc.h, g_t)
            v.tensor_scalar(arow[0:1, 3:4], gtanh[0:1, :], 0.5, 0.5, ALU.mult, ALU.add)
            v.drain()
            v.tensor_tensor(arow[0:1, 4:5], arow[0:1, 3:4], arow[0:1, 0:1], ALU.mult)
            v.drain()
            v.tensor_scalar(arow[0:1, 4:5], arow[0:1, 4:5], 1.4, 0.1, ALU.mult, ALU.add)
            v.drain()
            v.tensor_tensor(arow[0:1, 4:5], arow[0:1, 4:5], arow[0:1, 0:1], ALU.subtract)
            v.tensor_scalar(arow[0:1, 5:6], arow[0:1, 0:1], 0.3, None, ALU.mult)
            v.drain()
            v.tensor_tensor(abrow[0:1, 0:1], arow[0:1, 4:5], arow[0:1, 5:6], ALU.add)
            v.tensor_copy(abrow[0:1, 1:2], arow[0:1, 5:6])
            v.tensor_scalar(abrow[0:1, 2:3], arow[0:1, 1:2], 0.5, None, ALU.mult)
            v.drain()
            v.tensor_scalar(abrow[0:1, 3:4], abrow[0:1, 0:1], -1.0, None, ALU.mult)
            v.tensor_scalar(lnb[0:1, 0:128], onesf[0:1, :], abrow[0:1, 1:2], None, ALU.mult)
            v.drain()
            nc.vector.engine_nop().then_inc(s_gc.h, 1)
        g_t = emit_gc(dv12, "dve")

        def pe11(te, g_t=g_t):
            te.wait_ge(s_gc.h, g_t)
            te.matmul(pR[:, 28:30], onesf[0:1, :], abrow[0:1, 0:2], start=True, stop=True,
                      skip_group_check=True)
            te.matmul(pR[:, 30:32], onesf[0:1, :], abrow[0:1, 2:4],
                      start=True, stop=True,
                      skip_group_check=True).then_inc(s_gc.h, 1)
        g_t = emit_gc(pe11, "pe")

        g_t = emit_gc(lambda v, g_t=g_t: (
            v.wait_ge(s_gc.h, g_t),
            v.tensor_copy(ccols[:, 0:2], pR[:, 28:30]),
            v.tensor_copy(ccols[:, 2:4], pR[:, 30:32]),
            v.drain(),
            nc.vector.engine_nop().then_inc(s_gc.h, 1),
        ), "dve")
        gcnet_done = g_t

        widen_pend = {}
        for hf_ in range(2):
            widen_pend[hf_] = True

        for T in range(NFT):
            par_T = T % 2
            oT = (outT0, outT1)[par_T]
            if T == 0:
                for hf_ in range(2):
                    emit_widen(0, hf_)
            for cyc in range(NCY):
                for hf in range(2):
                    for dch in range(8):
                        bank = 2 * hf + dch % 2
                        bfree = l_bank_free[bank]

                        def peL1(te, dch=dch, hf=hf, bank=bank, bfree=bfree,
                                 sr=shad_ready_h[hf], sb0=shad_base_h[hf],
                                 hsem=shad_sems[hf],
                                 cr=cast_ready_h[hf], usecast=(cyc > 0),
                                 first=(dch == 0),
                                 needsum=(T == 0 and cyc == 0), sum_f=sum_final_c):
                            if first:
                                if usecast:
                                    te.wait_ge(s_tw.h, cr)
                                else:
                                    te.wait_ge(hsem.h, sr)
                                if needsum:
                                    te.wait_ge(s_sum.h, sum_f)
                            if bfree > 0:
                                te.wait_ge(s_h.h, bfree)
                            for k in range(8):
                                mm = te.matmul(SLOT[bank][:, 0:256],
                                               cw1sb[:, (k * 8 + dch) * 128: (k * 8 + dch) * 128 + 128],
                                               outTbf[:, k * 512 + hf * 256: k * 512 + hf * 256 + 256],
                                               start=(k == 0), stop=(k == 7))
                            mm.then_inc(s_l.h, 1)
                        P.pe.append(peL1)
                        l_t = s_l.inc()
                        hg = h1_gate[hf]

                        def acL1(a, l_t=l_t, dch=dch, hf=hf, bank=bank, hg=hg):
                            a.wait_ge(s_l.h, l_t)
                            if hg > 0:
                                a.wait_ge(s_l.h, hg)
                            a.activation(h1T[:, dch * 512 + hf * 256: dch * 512 + hf * 256 + 256],
                                         SLOT[bank][:, 0:256], AF.Gelu,
                                         bias=cb1sb[:, dch:dch + 1]).then_inc(s_h.h, 1)
                        P.act.append(acL1)
                        l_bank_free[bank] = s_h.inc()
                    l1_after[hf] = s_l.n
                    h1_done = s_h.n
                    if cyc == NCY - 1 and T + 1 < NFT:
                        emit_init(T + 1, hf)

                    for dch in range(4):
                        q = dch % 2
                        bfree = l2_bank_free[q]

                        def peL2(te, dch=dch, hf=hf, q=q, h1_done=h1_done, bfree=bfree):
                            if dch == 0:
                                te.wait_ge(s_h.h, h1_done)
                            if bfree > 0:
                                te.wait_ge(s_h.h, bfree)
                            for k in range(8):
                                mm = te.matmul(pQK[:, q * 512: q * 512 + 256],
                                               cw2sb[:, (k * 4 + dch) * 128: (k * 4 + dch) * 128 + 128],
                                               h1T[:, k * 512 + hf * 256: k * 512 + hf * 256 + 256],
                                               start=(k == 0), stop=(k == 7))
                            mm.then_inc(s_l.h, 1)
                        P.pe.append(peL2)
                        l_t = s_l.inc()
                        hg = h2_gate[hf]

                        def acL2(a, l_t=l_t, dch=dch, hf=hf, q=q, hg=hg):
                            a.wait_ge(s_l.h, l_t)
                            if hg > 0:
                                a.wait_ge(s_l.h, hg)
                            a.activation(h2T[:, dch * 512 + hf * 256: dch * 512 + hf * 256 + 256],
                                         pQK[:, q * 512: q * 512 + 256],
                                         AF.Gelu, bias=cb2sb[:, dch:dch + 1]).then_inc(s_h.h, 1)
                        P.act.append(acL2)
                        l2_bank_free[q] = s_h.inc()
                    h1_gate[hf] = s_l.n
                    h2_done = s_h.n

                    for dch in range(2):
                        bank = 2 * hf + dch
                        bfree = l_bank_free[bank]

                        def peL3(te, dch=dch, hf=hf, bank=bank, h2_done=h2_done, bfree=bfree):
                            if dch == 0:
                                te.wait_ge(s_h.h, h2_done)
                            if bfree > 0:
                                te.wait_ge(s_h.h, bfree)
                            for k in range(4):
                                mm = te.matmul(SLOT[bank][:, 0:256],
                                               cw3sb[:, (k * 2 + dch) * 128: (k * 2 + dch) * 128 + 128],
                                               h2T[:, k * 512 + hf * 256: k * 512 + hf * 256 + 256],
                                               start=(k == 0), stop=(k == 3))
                            mm.then_inc(s_l.h, 1)
                        P.pe.append(peL3)
                        l_t = s_l.inc()
                        hg = h3_gate[hf]

                        def acL3(a, l_t=l_t, dch=dch, hf=hf, bank=bank, hg=hg):
                            a.wait_ge(s_l.h, l_t)
                            if hg > 0:
                                a.wait_ge(s_l.h, hg)
                            a.activation(h3T[:, dch * 512 + hf * 256: dch * 512 + hf * 256 + 256],
                                         SLOT[bank][:, 0:256], AF.Gelu,
                                         bias=cb3sb[:, dch:dch + 1]).then_inc(s_h.h, 1)
                        P.act.append(acL3)
                        l_bank_free[bank] = s_h.inc()
                    h3_done = s_h.n

                    def peT(te, hf=hf, h3_done=h3_done, tfree=tf_free[0],
                            efree=epi_free[0], needgc=(T == 0 and cyc == 0), g_t=gcnet_done):
                        te.wait_ge(s_h.h, h3_done)
                        if needgc:
                            te.wait_ge(s_gc.h, g_t)
                        if tfree > 0:
                            te.wait_ge(s_h.h, tfree)
                        if efree > 0:
                            te.wait_ge(s_aoev.h, efree)
                        for k in range(2):
                            mm = te.matmul(pR[0:1, 0:256], cw4sb[:, k:k + 1],
                                           h3T[:, k * 512 + hf * 256: k * 512 + hf * 256 + 256],
                                           start=(k == 0), stop=(k == 1),
                                           skip_group_check=True)
                        mm.then_inc(s_l.h, 1)
                    P.pe.append(peT)
                    h3_gate[hf] = s_l.n
                    tf_t = s_l.inc()

                    def acT(a, tf_t=tf_t):
                        a.wait_ge(s_l.h, tf_t)
                        a.activation(tanhrow[0:1, :], pR[0:1, 0:256], AF.Tanh,
                                     scale=0.5, bias=abrow[0:1, 2:3]).then_inc(s_h.h, 1)
                    P.act.append(acT)
                    th_t = s_h.inc()

                    def peB(te, th_t=th_t):
                        te.wait_ge(s_h.h, th_t)
                        te.matmul(pA[:, 0:256], lnb[0:1, 0:128], tanhrow[0:1, :],
                                  start=True, stop=True,
                                  skip_group_check=True).then_inc(s_l.h, 1)
                    P.pe.append(peB)
                    bc_t = s_l.inc()

                    cg = c4n_free[0]

                    def acC4(a, bc_t=bc_t, cg=cg):
                        a.wait_ge(s_l.h, bc_t)
                        if cg > 0:
                            a.wait_ge(s_tw.h, cg)
                        a.activation(C4N[:, 256:512], pA[:, 0:256], AF.Identity,
                                     bias=ccols[:, 0:1])
                        a.activation(bass.AP(C4N, 0, [[768, 128], [512, 2], [1, 256]]),
                                     bass.AP(pA, 0, [[512, 128], [0, 2], [1, 256]]),
                                     AF.Identity, scale=-1.0,
                                     bias=ccols[:, 3:4]).then_inc(s_h.h, 1)
                    P.act.append(acC4)
                    tf_free[0] = s_h.inc()
                    c4t = tf_free[0]

                    o = hf * 256

                    def dvP(v, o=o, oT=oT, c4t=c4t):
                        v.wait_ge(s_h.h, c4t)
                        for g in range(4):
                            v.tensor_tensor(
                                bass.AP(Ptw, g * 512, [[2048, 128], [256, 2], [1, 256]]),
                                bass.AP(oT, g * 512 + o, [[4096, 128], [2048, 2], [1, 256]]),
                                bass.AP(C4N, 0 if g < 2 else 256,
                                        [[768, 128], [256, 2], [1, 256]]),
                                ALU.mult)
                        v.drain()
                        nc.vector.engine_nop().then_inc(s_tw.h, 1)
                    P.dve.append(dvP)
                    c4n_free[0] = s_tw.inc()

                    def dvU(v, o=o, oT=oT, hf=hf, docast=(cyc < NCY - 1)):
                        v.tensor_tensor(
                            bass.AP(oT, o, [[4096, 128], [512, 2], [1, 256]]),
                            bass.AP(Ptw, 256, [[2048, 128], [512, 2], [1, 256]]),
                            bass.AP(oT, o, [[4096, 128], [512, 2], [1, 256]]),
                            ALU.add)
                        v.tensor_tensor(
                            bass.AP(oT, 2048 + o, [[4096, 128], [512, 2], [1, 256]]),
                            bass.AP(Ptw, 0, [[2048, 128], [512, 2], [1, 256]]),
                            bass.AP(oT, 2048 + o, [[4096, 128], [512, 2], [1, 256]]),
                            ALU.add)
                        v.tensor_tensor(
                            bass.AP(oT, 1024 + o, [[4096, 128], [512, 2], [1, 256]]),
                            bass.AP(Ptw, 1280, [[2048, 128], [512, 2], [1, 256]]),
                            bass.AP(oT, 1024 + o, [[4096, 128], [512, 2], [1, 256]]),
                            ALU.add)
                        v.tensor_tensor(
                            bass.AP(oT, 3072 + o, [[4096, 128], [512, 2], [1, 256]]),
                            bass.AP(Ptw, 1024, [[2048, 128], [512, 2], [1, 256]]),
                            bass.AP(oT, 3072 + o, [[4096, 128], [512, 2], [1, 256]]),
                            ALU.add)
                        v.drain()
                        if docast:
                            v.tensor_copy(
                                bass.AP(outTbf, o, [[4096, 128], [512, 8], [1, 256]]),
                                bass.AP(oT, o, [[4096, 128], [512, 8], [1, 256]]))
                            v.drain()
                        nc.vector.engine_nop().then_inc(s_tw.h, 1)
                    P.dve.append(dvU)
                    cast_ready_h[hf] = s_tw.inc()

            x1base = T * 4096
            if T + 1 < NFT:
                emit_widen(T + 1, 0)
            ep_first = True
            ep_bank = [0]
            for st in range(4):
                par = st % 2
                stile = sumtile if par == 0 else sumtile2
                gtok = T * 4 + st

                def peU(te, st=st, oT=oT, ep_first=ep_first, tfree=tf_free[0],
                        ebank=ep_bank[0],
                        twA=cast_ready_h[0], twB=cast_ready_h[1]):
                    if ep_first:
                        te.wait_ge(s_h.h, tfree)
                        te.wait_ge(s_tw.h, twA)
                        te.wait_ge(s_tw.h, twB)
                    if ebank > 0:
                        te.wait_ge(s_aoev.h, ebank)
                    for c in range(8):
                        mm = te.matmul((pA if c < 4 else pR)[:, (c % 4) * 128: (c % 4) * 128 + 128],
                                       oT[:, c * 512 + st * 128: c * 512 + st * 128 + 128],
                                       idf32[:, :], is_transpose=True,
                                       start=True, stop=True,
                                       skip_group_check=True)
                    mm.then_inc(s_l.h, 1)
                P.pe.append(peU)
                ep_first = False
                u_t = s_l.inc()

                def acU(a, u_t=u_t, stile=stile, sfree=stile_free[par],
                        gcg=(gcnet_done if (T == 0 and st == 0) else 0)):
                    a.wait_ge(s_l.h, u_t)
                    if gcg > 0:
                        a.wait_ge(s_gc.h, gcg)
                    if sfree > 0:
                        a.wait_ge(s_st.h, sfree)
                    a.activation(stile[:, 0:512], pA[:, :], AF.Copy)
                    a.activation(stile[:, 512:1024], pR[:, :],
                                 AF.Copy).then_inc(s_aoev.h, 1)
                P.act.append(acU)
                ue_t = s_aoev.inc()
                ep_bank[0] = ue_t
                epi_free[0] = ue_t
                outT_free[par_T] = ue_t

                def dvZ(v, ue_t=ue_t, stile=stile, base=x1base + st * 1024, par=par):
                    v.wait_ge(s_aoev.h, ue_t)
                    v.tensor_tensor(stile[:, :], x1[:, base: base + 1024],
                                    stile[:, :], ALU.add)
                    v.drain()
                    v.bn_stats(stats[:, 0:6], stile[:, 0:512])
                    v.bn_stats(stats[:, 6:12], stile[:, 512:1024])
                    v.drain()
                    v.bn_aggr((aggr if par == 0 else aggr2)[:, :], stats[:, :])
                    v.drain()
                    nc.vector.engine_nop().then_inc(s_st.h, 1)
                P.dve.append(dvZ)
                st_t = s_st.inc()
                stile_free[par] = st_t

                def acZ(a, st_t=st_t, par=par):
                    a.wait_ge(s_st.h, st_t)
                    a.activation(lncol[:, par * 2: par * 2 + 1],
                                 (aggr if par == 0 else aggr2)[:, 1:2],
                                 AF.Ln, bias=epscol[:, :])
                    a.drain()
                    a.activation(lncol[:, par * 2 + 1: par * 2 + 2],
                                 lncol[:, par * 2: par * 2 + 1],
                                 AF.Exp, scale=-0.5).then_inc(s_rstd.h, 1)
                P.act.append(acZ)
                rstd_t = s_rstd.inc()

                osem = od_sems[gtok % 2]
                x2f = x2_free[gtok % 2]

                def dvX2(v, rstd_t=rstd_t, gtok=gtok, x2f=x2f, osem=osem,
                         stile=stile, par=par):
                    v.wait_ge(s_rstd.h, rstd_t)
                    if x2f > 0:
                        v.wait_ge(osem.h, x2f)
                    v.tensor_scalar(x2slab[:, (gtok % 2) * 1024: (gtok % 2) * 1024 + 1024],
                                    stile[:, :],
                                    (aggr if par == 0 else aggr2)[:, 0:1],
                                    lncol[:, par * 2 + 1: par * 2 + 2],
                                    ALU.subtract, ALU.mult)
                    v.drain()
                    nc.vector.engine_nop().then_inc(s_x2.h, 1)
                P.dve.append(dvX2)
                x2_t = s_x2.inc()

                def spO(s, x2_t=x2_t, gtok=gtok, b=b, osem=osem):
                    s.wait_ge(s_x2.h, x2_t)
                    s.dma_start(dout.ap()[b * S + gtok * 128: b * S + (gtok + 1) * 128, :],
                                x2slab[:, (gtok % 2) * 1024: (gtok % 2) * 1024 + 1024]).then_inc(osem.h, 16)
                P.sp.append(spO)
                od_t = osem.inc(16)
                x2_free[gtok % 2] = od_t
                if st == 1 and T + 1 < NFT:
                    emit_widen(T + 1, 1)

    P = new_block()
    odA, odB = s_od.n, s_odB.n

    def sp_final(s, odA=odA, odB=odB):
        s.wait_ge(s_od.h, odA)
        s.wait_ge(s_odB.h, odB)
    P.sp.append(sp_final)

    for prog in blocks:
        with nc.Block() as block:
            if prog.pe:
                @block.tensor
                def _(te, prog=prog):
                    for f in prog.pe:
                        f(te)
            if prog.act:
                @block.scalar
                def _(a, prog=prog):
                    for f in prog.act:
                        f(a)
            if prog.dve:
                @block.vector
                def _(v, prog=prog):
                    for f in prog.dve:
                        f(v)
            if prog.sp:
                @block.sync
                def _(s, prog=prog):
                    for f in prog.sp:
                        f(s)
            if prog.gp:
                @block.gpsimd
                def _(g, prog=prog):
                    for f in prog.gp:
                        f(g)

    es.close()
    return nc


def _build_ex4():
    r = np.arange(128)
    m = np.arange(128)
    ex = np.zeros((512, 2048), dtype=np.float32)
    for v in range(4):
        for hh in range(2):
            for k in range(8):
                tgt = hh * 16 + 2 * k + (m >= 64).astype(np.int64)
                blk = ((r[:, None] % 32 == tgt[None, :]) & (r[:, None] // 32 == v))
                ex[v * 128: v * 128 + 128, (hh * 8 + k) * 128: (hh * 8 + k) * 128 + 128] = blk
    return np.ascontiguousarray(ex)


def kernel(**inputs):
    x = np.ascontiguousarray(np.asarray(inputs["L0_attn_x"], dtype=np.float32))
    assert x.shape == (16, 2048, 1024)

    def T(a):
        return np.ascontiguousarray(np.asarray(a, dtype=np.float32).T)

    def C(a):
        return np.ascontiguousarray(np.asarray(a, dtype=np.float32))

    com = {
        "wqT": T(inputs["L0_attn_Wq"]), "wkT": T(inputs["L0_attn_Wk"]),
        "wvT": T(inputs["L0_attn_Wv"]), "wfcT": T(inputs["L0_attn_Wfc"]),
        "cw1": C(inputs["L0_ffn_cn_w1"]), "cw2": C(inputs["L0_ffn_cn_w2"]),
        "cw3": C(inputs["L0_ffn_cn_w3"]), "cw4": C(inputs["L0_ffn_cn_w4"]),
        "cb1": C(inputs["L0_ffn_cn_b1"]), "cb2": C(inputs["L0_ffn_cn_b2"]),
        "cb3": C(inputs["L0_ffn_cn_b3"]), "cb4": C(inputs["L0_ffn_cn_b4"]),
        "gw1": C(inputs["L0_ffn_gc_w1"]), "gw2": C(inputs["L0_ffn_gc_w2"]),
        "gw3": C(inputs["L0_ffn_gc_w3"]),
        "gb1": C(inputs["L0_ffn_gc_b1"]), "gb2": C(inputs["L0_ffn_gc_b2"]),
        "gb3": C(inputs["L0_ffn_gc_b3"]),
        "ar": C(inputs["L0_ffn_adaptive_range"]),
        "ex4": _build_ex4(),
    }
    nc = build_nc()
    in_maps = []
    for c in range(NCORES):
        m = dict(com)
        m["x"] = np.ascontiguousarray(x[2 * c: 2 * c + 2].reshape(2 * 2048, 1024))
        in_maps.append(m)
    res = run_bass_kernel_spmd(nc, in_maps, core_ids=list(range(NCORES)))
    out = np.empty((16, 2048, 1024), dtype=np.float32)
    for c in range(NCORES):
        out[2 * c: 2 * c + 2] = res.results[c]["out"].reshape(2, 2048, 1024)
    return out


# version 33 (final)
# speedup vs baseline: 1.1852x; 1.1852x over previous
import numpy as np
from contextlib import ExitStack

import concourse.bass as bass
import concourse.mybir as mybir
from concourse.bass_utils import run_bass_kernel_spmd

F32 = mybir.dt.float32
BF16 = mybir.dt.bfloat16
AF = mybir.ActivationFunctionType
ALU = mybir.AluOpType

NCORES = 8
B_LOC = 2
S = 2048
D = 1024
EXP_SCALE = 0.125
EXP_BIAS = -10.0
LN_EPS = 1e-5
NTT = 16
NSB = 4
NFT = 4
NCY = 3


class Sem:
    def __init__(self, nc, es, name):
        self.h = es.enter_context(nc.semaphore(name))
        self.n = 0

    def inc(self, k=1):
        self.n += k
        return self.n


DEBUG = [False]


def build_nc():
    nc = bass.Bass("TRN2", target_bir_lowering=False)
    es = ExitStack()

    dx = nc.dram_tensor("x", [B_LOC * S, D], F32, kind="ExternalInput")
    dex4 = nc.dram_tensor("ex4", [512, 2048], F32, kind="ExternalInput")
    dout = nc.dram_tensor("out", [B_LOC * S, D], F32, kind="ExternalOutput")
    dbg_x1 = nc.dram_tensor("dbg_x1", [S, D], F32, kind="ExternalOutput")
    dbg_ao = nc.dram_tensor("dbg_ao", [128, 16384], BF16, kind="ExternalOutput")
    dwq = nc.dram_tensor("wqT", [D, D], F32, kind="ExternalInput")
    dwk = nc.dram_tensor("wkT", [D, D], F32, kind="ExternalInput")
    dwv = nc.dram_tensor("wvT", [D, D], F32, kind="ExternalInput")
    dwfc = nc.dram_tensor("wfcT", [D, D], F32, kind="ExternalInput")
    dcw1 = nc.dram_tensor("cw1", [D, D], F32, kind="ExternalInput")
    dcw2 = nc.dram_tensor("cw2", [D, 512], F32, kind="ExternalInput")
    dcw3 = nc.dram_tensor("cw3", [512, 256], F32, kind="ExternalInput")
    dcw4 = nc.dram_tensor("cw4", [256, 1], F32, kind="ExternalInput")
    dcb1 = nc.dram_tensor("cb1", [D], F32, kind="ExternalInput")
    dcb2 = nc.dram_tensor("cb2", [512], F32, kind="ExternalInput")
    dcb3 = nc.dram_tensor("cb3", [256], F32, kind="ExternalInput")
    dcb4 = nc.dram_tensor("cb4", [1], F32, kind="ExternalInput")
    dgw1 = nc.dram_tensor("gw1", [D, 512], F32, kind="ExternalInput")
    dgw2 = nc.dram_tensor("gw2", [512, 256], F32, kind="ExternalInput")
    dgw3 = nc.dram_tensor("gw3", [256, 1], F32, kind="ExternalInput")
    dgb1 = nc.dram_tensor("gb1", [512], F32, kind="ExternalInput")
    dgb2 = nc.dram_tensor("gb2", [256], F32, kind="ExternalInput")
    dgb3 = nc.dram_tensor("gb3", [1], F32, kind="ExternalInput")
    dar = nc.dram_tensor("ar", [1], F32, kind="ExternalInput")

    BASE = (nc.sbuf_base + 63) // 64 * 64
    TOP = nc.sbuf_top
    off = [0]

    def at(name, shape, dtype, align=64):
        o = (off[0] + align - 1) // align * align
        t = nc.alloc_sbuf_tensor_at(name, shape, dtype, offset=BASE + o)
        nbytes = 1
        for s_ in shape[1:]:
            nbytes *= s_
        nbytes *= mybir.dt.size(dtype)
        off[0] = o + nbytes
        return t

    def at_fixed(name, shape, dtype, o):
        return nc.alloc_sbuf_tensor_at(name, shape, dtype, offset=BASE + o)

    onesb = at("onesb", [128, 128], BF16)
    onesf = at("onesf", [1, 128], F32)
    stats = at("stats", [128, 12], F32)
    aggr = at("aggr", [128, 2], F32)
    aggr2 = at("aggr2", [128, 2], F32)
    lncol = at("lncol", [128, 4], F32)
    ccols = at("ccols", [128, 4], F32)
    c4 = at("c4", [128, 4], F32)
    negc4 = at("negc4", [128, 4], F32)
    tanh4 = at("tanh4", [128, 4], F32)
    gcT = at("gcT", [128, 8], BF16)
    g1T = at("g1T", [128, 4], BF16)
    g2T = at("g2T", [128, 2], BF16)
    arow = at("arow", [1, 8], F32)
    abrow = at("abrow", [1, 4], F32)
    g3pre = at("g3pre", [1, 1], F32)
    gtanh = at("gtanh", [1, 1], F32)
    expb = at("expb", [128, 1], F32)
    epscol = at("epscol", [128, 1], F32)
    rcol = at("rcol", [128, 2], F32)
    iwork = at("iwork", [128, 128], F32)
    iwork2 = at("iwork2", [128, 128], F32)
    indt = at("indt", [128, 16 * 32], BF16)
    expf4_0 = at("expf4_0", [128, 16 * 128], BF16)
    expf4_3 = at("expf4_3", [128, 16 * 128], BF16)
    redda = at("redda", [128, 128], BF16)
    reddb = at("reddb", [128, 128], BF16)
    rind = at("rind", [128, 8], BF16)
    e8 = at("e8", [128, 128], F32)
    gw3sb = at("gw3sb", [128, 2], BF16)
    lnb = at("lnb", [8, 512], F32)
    idf32 = at("idf32", [128, 128], F32)
    onec = at("onec", [128, 1], F32)
    tanhrow = at("tanhrow", [1, 256], F32)
    c4row = at("c4row", [1, 256], F32)
    assert off[0] <= 18000, off[0]

    off[0] = 18048
    cw1sb = at("cw1sb", [128, 64 * 128], BF16)
    cw2sb = at("cw2sb", [128, 32 * 128], BF16)
    cw3sb = at("cw3sb", [128, 8 * 128], BF16)
    cw4sb = at("cw4sb", [128, 2], BF16)
    cb1sb = at("cb1sb", [128, 8], F32)
    cb2sb = at("cb2sb", [128, 4], F32)
    cb3sb = at("cb3sb", [128, 2], F32)
    gb1row = at("gb1row", [1, 512], F32)
    gb2row = at("gb2row", [1, 256], F32)
    GW_OFF = (off[0] + 63) // 64 * 64
    off[0] = GW_OFF
    gw1sb = at("gw1sb", [128, 8 * 512], BF16)
    gw2sb = at("gw2sb", [128, 4 * 256], BF16)
    sumtile = at_fixed("sumtile", [128, 1024], F32, GW_OFF)
    sumtile2 = at_fixed("sumtile2", [128, 1024], F32, GW_OFF + 4096)
    expf4_1 = at_fixed("expf4_1", [128, 16 * 128], BF16, GW_OFF)
    expf4_2 = at_fixed("expf4_2", [128, 16 * 128], BF16, GW_OFF + 4096)
    x1bfgc = at_fixed("x1bfgc", [128, 1024], BF16, GW_OFF + 8192)
    WQ_OFF = (off[0] + 63) // 64 * 64
    off[0] = WQ_OFF
    wqsb = at("wqsb", [128, 64 * 128], BF16)
    wksb = at("wksb", [128, 64 * 128], BF16)
    wvsb = at("wvsb", [128, 64 * 128], BF16)
    x2slab = at_fixed("x2slab", [128, 2 * 1024], F32, WQ_OFF)
    xstg = at_fixed("xstg", [128, 1024], BF16, WQ_OFF + 8192)
    outbf = at_fixed("outbf", [128, 1024], BF16, WQ_OFF + 10240)
    outbf2 = at_fixed("outbf2", [128, 1024], BF16, WQ_OFF + 12288)
    WK_OFF = WQ_OFF + 16384
    x0f32 = at_fixed("x0f32", [128, 2 * 1024], F32, WK_OFF)
    h2T = at_fixed("h2T", [128, 4 * 512], BF16, WK_OFF + 8192)
    h3T = at_fixed("h3T", [128, 2 * 512], BF16, WK_OFF + 12288)
    WV_OFF = WK_OFF + 16384
    wfcsb = at_fixed("wfcsb", [128, 8 * 1024], BF16, WV_OFF)

    OV_OFF = (off[0] + 63) // 64 * 64
    off[0] = OV_OFF
    x0tok = at("x0tok", [128, 2 * 1024], BF16)
    x0T = at("x0T", [128, 2 * 8 * 512], BF16)
    qtsb = at("qtsb", [128, 8 * 512], BF16)
    ktsb = at("ktsb", [128, 8 * 512], BF16)
    vtsb = at("vtsb", [128, 8 * 512], BF16)
    kr = at("kr", [128, 2 * 512], BF16)
    prodb = at("prodb", [128, 2 * 2048], BF16)
    praw = at("praw", [128, 1024], BF16)
    pfm = at("pfm", [128, 1024], BF16)
    pexpsb = at("pexpsb", [128, 4 * 512], BF16)
    prdsb = at("prdsb", [128, 4 * 512], BF16)
    rcpf1 = at("rcpf1", [128, 512], F32)
    rcpf2 = at("rcpf2", [128, 512], F32)
    OVC_END = off[0]
    x1 = at_fixed("x1", [128, 16 * 1024], F32, OV_OFF)
    X1_END = OV_OFF + 16 * 1024 * 4

    AO_OFF = (max(OVC_END, X1_END) + 63) // 64 * 64
    off[0] = AO_OFF
    attnoutT = at("attnoutT", [128, 8 * 2048], BF16)
    AO_END = off[0]
    assert X1_END <= AO_OFF, (X1_END, AO_OFF)
    OT_OFF = AO_OFF
    outT0 = at_fixed("outT0", [128, 4 * 1024], F32, OT_OFF)
    outT1 = at_fixed("outT1", [128, 4 * 1024], F32, WV_OFF)
    outTbf = at_fixed("outTbf", [128, 8 * 512], BF16, OT_OFF + 16384)
    h1T = at_fixed("h1T", [128, 8 * 512], BF16, OT_OFF + 16384 + 8192)
    Ptw = at_fixed("Ptw", [128, 2048], F32, WK_OFF)
    C4N = at_fixed("C4N", [128, 768], F32, WQ_OFF + 10240)
    FFN_END = OT_OFF + 16384 + 8192 + 8192
    assert FFN_END <= AO_END, (FFN_END, AO_END)
    gcrow = at_fixed("gcrow", [1, 1024], BF16, OT_OFF)
    growsf = at_fixed("growsf", [1, 1024], F32, OT_OFF + 2048)
    g1pre = at_fixed("g1pre", [1, 512], F32, OT_OFF + 6144)
    g1b = at_fixed("g1b", [1, 512], BF16, OT_OFF + 8192)
    g2pre = at_fixed("g2pre", [1, 256], F32, OT_OFF + 9216)
    g2b = at_fixed("g2b", [1, 256], BF16, OT_OFF + 10240)

    LIMIT = TOP - BASE
    assert max(AO_END, FFN_END) <= LIMIT, (AO_END, FFN_END, LIMIT)

    pBIG = es.enter_context(nc.psum_tensor("pBIG", [128, 2048], F32))
    pQK = es.enter_context(nc.psum_tensor("pQK", [128, 1024], F32))
    pR = es.enter_context(nc.psum_tensor("pR", [128, 512], F32))
    pA = es.enter_context(nc.psum_tensor("pA", [128, 512], F32))
    pS = pBIG
    pX = pBIG[:, 1024:2048]
    SLOT = [pBIG[:, 0:512], pBIG[:, 512:1024], pBIG[:, 1024:1536], pBIG[:, 1536:2048]]

    sems = {}
    for name in ("s_winit s_const s_x0tok s_x0T s_x0f s_peq s_qkv s_prod s_sacc "
                 "s_exp2 s_rsum s_rcp2 s_rexp s_pnorm s_pexp s_pev s_prd s_pacc "
                 "s_aoev s_fc s_sum s_st s_rstd s_x1 s_gcs s_gc s_shad s_l s_h "
                 "s_tw s_x2 s_od s_odB s_x0tB s_x0fB s_dmp s_wq s_wf2 s_wgw s_wk s_wv s_shB s_krc s_ex4").split():
        sems[name] = Sem(nc, es, name)
    (s_winit, s_const, s_x0tok, s_x0T, s_x0f, s_peq, s_qkv, s_prod, s_sacc,
     s_exp2, s_rsum, s_rcp2, s_rexp, s_pnorm, s_pexp, s_pev, s_prd, s_pacc,
     s_aoev, s_fc, s_sum, s_st, s_rstd, s_x1, s_gcs, s_gc, s_shad, s_l, s_h,
     s_tw, s_x2, s_od, s_odB, s_x0tB, s_x0fB, s_dmp, s_wq, s_wf2, s_wgw,
     s_wk, s_wv, s_shB, s_krc, s_ex4) = (sems[n] for n in (
         "s_winit s_const s_x0tok s_x0T s_x0f s_peq s_qkv s_prod s_sacc s_exp2 "
         "s_rsum s_rcp2 s_rexp s_pnorm s_pexp s_pev s_prd s_pacc s_aoev s_fc "
         "s_sum s_st s_rstd s_x1 s_gcs s_gc s_shad s_l s_h s_tw s_x2 s_od "
         "s_odB s_x0tB s_x0fB s_dmp s_wq s_wf2 s_wgw s_wk s_wv s_shB s_krc s_ex4").split())
    x0tok_sems = [s_x0tok, s_x0tB]
    shad_sems = [None, None]
    x0f_sems = [s_x0f, s_x0fB]
    od_sems = [s_od, s_odB]

    class Prog:
        def __init__(self):
            self.pe = []
            self.act = []
            self.dve = []
            self.sp = []
            self.gp = []

    blocks = []

    def new_block():
        p = Prog()
        blocks.append(p)
        return p

    P = new_block()

    def gp_init(g):
        g.dma_start(cw1sb[:, :], bass.AP(dcw1, 0, [[1024, 128], [1024 * 128, 8], [128, 8], [1, 128]])).then_inc(s_winit.h, 16)
        g.dma_start(cw2sb[:, :], bass.AP(dcw2, 0, [[512, 128], [512 * 128, 8], [128, 4], [1, 128]])).then_inc(s_winit.h, 16)
        g.dma_start(cw3sb[:, :], bass.AP(dcw3, 0, [[256, 128], [256 * 128, 4], [128, 2], [1, 128]])).then_inc(s_winit.h, 16)
        g.dma_start(cw4sb[:, :], bass.AP(dcw4, 0, [[1, 128], [128, 2], [1, 1]]),
                    allow_slow_non_contiguous=True).then_inc(s_winit.h, 16)
        g.dma_start(cb1sb[:, :], bass.AP(dcb1, 0, [[1, 128], [128, 8]]),
                    allow_slow_non_contiguous=True).then_inc(s_winit.h, 16)
        g.dma_start(cb2sb[:, :], bass.AP(dcb2, 0, [[1, 128], [128, 4]]),
                    allow_slow_non_contiguous=True).then_inc(s_winit.h, 16)
        g.dma_start(cb3sb[:, :], bass.AP(dcb3, 0, [[1, 128], [128, 2]]),
                    allow_slow_non_contiguous=True).then_inc(s_winit.h, 16)
        g.dma_start(gb1row[0:1, :], bass.AP(dgb1, 0, [[512, 1], [1, 512]])).then_inc(s_winit.h, 16)
        g.dma_start(gb2row[0:1, :], bass.AP(dgb2, 0, [[256, 1], [1, 256]])).then_inc(s_winit.h, 16)
        g.dma_start(gw3sb[:, :], bass.AP(dgw3, 0, [[1, 128], [128, 2], [1, 1]]),
                    allow_slow_non_contiguous=True).then_inc(s_winit.h, 16)
        g.dma_start(arow[0:1, 0:1], bass.AP(dar, 0, [[1, 1], [1, 1]]),
                    allow_slow_non_contiguous=True).then_inc(s_winit.h, 16)
        g.dma_start(arow[0:1, 1:2], bass.AP(dcb4, 0, [[1, 1], [1, 1]]),
                    allow_slow_non_contiguous=True).then_inc(s_winit.h, 16)
        g.dma_start(arow[0:1, 2:3], bass.AP(dgb3, 0, [[1, 1], [1, 1]]),
                    allow_slow_non_contiguous=True).then_inc(s_winit.h, 16)
        g.dma_start(expf4_0[:, :], dex4.ap()[0:128, :]).then_inc(s_ex4.h, 16)
        g.dma_start(expf4_3[:, :], dex4.ap()[384:512, :]).then_inc(s_ex4.h, 16)
        g.iota(iwork[:, :], [[1, 128]], channel_multiplier=0, allow_small_or_imprecise_dtypes=True)
        g.iota(rcol[:, 0:1], [[0, 1]], channel_multiplier=1, allow_small_or_imprecise_dtypes=True)
        nc.gpsimd.engine_nop().then_inc(s_const.h, 1)
    P.gp.append(gp_init)
    s_ex4.inc(32)
    s_winit.inc(16 * 13)
    N_WINIT = s_winit.n
    s_const.inc()

    def ve_init(v):
        v.wait_ge(s_const.h, 1)
        v.memset(onesb[:, :], 1.0)
        v.memset(onesf[:, :], 1.0)
        v.memset(expb[:, :], EXP_BIAS)
        v.memset(epscol[:, :], LN_EPS)
        v.memset(onec[:, :], 1.0)
        v.tensor_scalar(rcol[:, 1:2], rcol[:, 0:1], 64.0, None, ALU.is_ge)
        v.drain()
        for t in range(16):
            v.tensor_scalar(iwork2[:, 0:1], rcol[:, 1:2], 16.0, float(t), ALU.mult, ALU.add)
            v.drain()
            v.tensor_scalar(iwork2[:, 32:64], iwork[:, 0:32], iwork2[:, 0:1], None, ALU.is_equal)
            v.drain()
            v.tensor_copy(indt[:, t * 32: t * 32 + 32], iwork2[:, 32:64])
            v.drain()
        v.tensor_scalar(iwork2[:, 0:1], rcol[:, 1:2], -64.0, rcol[:, 0:1], ALU.mult, ALU.add)
        v.drain()
        v.tensor_scalar(iwork2[:, 1:65], iwork[:, 0:64], iwork2[:, 0:1], None, ALU.is_equal)
        v.drain()
        v.memset(redda[:, :], 0.0)
        v.memset(reddb[:, :], 0.0)
        v.drain()
        v.tensor_copy(redda[:, 0:64], iwork2[:, 1:65])
        v.tensor_copy(reddb[:, 64:128], iwork2[:, 1:65])
        v.drain()
        v.tensor_scalar(iwork2[:, 0:1], rcol[:, 0:1], 1.0 / 16.0, None, ALU.mult)
        v.drain()
        v.tensor_scalar(iwork2[:, 8:16], iwork[:, 0:8], -1.0, iwork2[:, 0:1], ALU.mult, ALU.add)
        v.drain()
        v.tensor_scalar(iwork2[:, 16:24], iwork2[:, 8:16], 0.0, None, ALU.is_ge)
        v.tensor_scalar(iwork2[:, 24:32], iwork2[:, 8:16], 1.0, None, ALU.is_lt)
        v.drain()
        v.tensor_tensor(iwork2[:, 32:40], iwork2[:, 16:24], iwork2[:, 24:32], ALU.mult)
        v.drain()
        v.tensor_copy(rind[:, :], iwork2[:, 32:40])
        v.drain()
        v.memset(e8[:, :], 0.0)
        v.memset(rcpf1[:, :], 0.0)
        v.memset(rcpf2[:, :], 0.0)
        v.drain()
        v.tensor_scalar(iwork2[:, 0:128], iwork[:, 0:128], 1.0 / 16.0, None, ALU.mult)
        v.drain()
        v.tensor_scalar(iwork2[0:8, 0:128], iwork2[0:8, 0:128], rcol[0:8, 0:1], None, ALU.subtract)
        v.drain()
        v.tensor_scalar(iwork[0:8, 0:128], iwork2[0:8, 0:128], 0.0, None, ALU.is_ge)
        v.tensor_scalar(iwork2[0:8, 0:128], iwork2[0:8, 0:128], 1.0, None, ALU.is_lt)
        v.drain()
        v.tensor_tensor(e8[0:8, :], iwork[0:8, 0:128], iwork2[0:8, 0:128], ALU.mult)
        v.drain()
        nc.vector.engine_nop().then_inc(s_const.h, 1)
    P.dve.append(ve_init)
    s_const.inc()

    def gp_init2(g):
        g.wait_ge(s_const.h, 2)
        g.iota(iwork[:, :], [[1, 128]], channel_multiplier=0, allow_small_or_imprecise_dtypes=True)
        nc.gpsimd.engine_nop().then_inc(s_const.h, 1)
    P.gp.append(gp_init2)
    s_const.inc()

    def ve_init2(v):
        v.wait_ge(s_const.h, 3)
        nc.vector.engine_nop().then_inc(s_const.h, 1)
    P.dve.append(ve_init2)
    s_const.inc()

    def gp_init3(g):
        g.wait_ge(s_const.h, 4)
        g.iota(iwork[:, :], [[1, 128]], channel_multiplier=0, allow_small_or_imprecise_dtypes=True)
        nc.gpsimd.engine_nop().then_inc(s_const.h, 1)
    P.gp.append(gp_init3)
    s_const.inc()

    def ve_init3(v):
        v.wait_ge(s_const.h, 5)
        v.tensor_scalar(idf32[:, :], iwork[:, 0:128], rcol[:, 0:1], None, ALU.is_equal)
        v.drain()
        nc.vector.engine_nop().then_inc(s_const.h, 1)
    P.dve.append(ve_init3)
    CONST_DONE = s_const.inc()

    x2_free = [0, 0]
    KRC = [0]
    OBG = [0, 0]
    shad_sems[0] = s_shad
    shad_sems[1] = s_shB

    for b in range(B_LOC):
        P = new_block()

        wqkv_base = s_wq.n

        def gp_w(g, b=b, odA=s_od.n, odB=s_odB.n):
            if odA > 0:
                g.wait_ge(s_od.h, odA)
            if odB > 0:
                g.wait_ge(s_odB.h, odB)
            for dst, src, sem in ((wqsb, dwq, s_wq), (wksb, dwk, s_wk), (wvsb, dwv, s_wv)):
                g.dma_start(dst[:, :], bass.AP(src, 0, [[1024, 128], [1024 * 128, 8], [128, 8], [1, 128]])).then_inc(sem.h, 16)
            g.dma_start(expf4_1[:, :], dex4.ap()[128:256, :]).then_inc(s_ex4.h, 16)
            g.dma_start(expf4_2[:, :], dex4.ap()[256:384, :]).then_inc(s_ex4.h, 16)
        gp_w_pending = [gp_w]
        wq_tgts = [s_wq.inc(16), s_wk.inc(16), s_wv.inc(16)]
        wq_sems = [s_wq, s_wk, s_wv]
        ex4_tgt = s_ex4.inc(32)

        pend_pv = None
        qkv_bank_free = {0: 0, 1: 0}
        sacc_track = {0: 0, 1: 0}
        pvb_track = {0: 0, 1: 0, 2: 0}
        sl_prd = {0: 0, 1: 0}
        sl_pacc = {0: 0, 1: 0}
        ao_gate_v = 0
        exp_gate_prev = 0
        pnorm_gate_prev = 0
        aoev_prev = 0

        for sb_ in range(NSB):
            par = sb_ % 2
            for tt4 in range(4):
                tt = sb_ * 4 + tt4

                tsem = x0tok_sems[tt4 % 2]
                gate = s_x0T.n

                def gp2(g, b=b, tt=tt, tt4=tt4, gate=gate, tsem=tsem,
                        first=(sb_ == 0 and tt4 == 0 and b == 0)):
                    if first:
                        g.wait_ge(s_const.h, CONST_DONE)
                    if gate > 0:
                        g.wait_ge(s_x0T.h, gate)
                    g.dma_start(x0tok[:, (tt4 % 2) * 1024: (tt4 % 2) * 1024 + 1024],
                                dx.ap()[b * S + tt * 128: b * S + (tt + 1) * 128, :]).then_inc(tsem.h, 16)
                P.gp.append(gp2)
                if gp_w_pending and sb_ == 0 and tt4 == 1:
                    P.gp.append(gp_w_pending.pop())
                tok_t = tsem.inc(16)

                x0T_reuse = 0 if sb_ < 2 else peq_after_sb[sb_ - 2]

                def sp(s, tt4=tt4, tok_t=tok_t, par=par, x0T_reuse=x0T_reuse,
                       tsem=tsem):
                    s.wait_ge(tsem.h, tok_t)
                    if x0T_reuse > 0:
                        s.wait_ge(s_peq.h, x0T_reuse)
                    for c in range(8):
                        s.dma_start(x0T[:, par * 4096 + c * 512 + tt4 * 128: par * 4096 + c * 512 + tt4 * 128 + 128],
                                    x0tok[:, (tt4 % 2) * 1024 + c * 128: (tt4 % 2) * 1024 + c * 128 + 128],
                                    transpose=True).then_inc(s_x0T.h, 16)
                P.sp.append(sp)
                s_x0T.inc(16 * 8)
            x0T_tgt = s_x0T.n
            if sb_ == 0:
                peq_after_sb = {}

            qk_prev_consumed = 0 if sb_ == 0 else sacc_final
            v_prev_consumed = 0 if sb_ == 0 else prd_final
            QPE, QAC, VPE, VAC = [], [], [], []
            qkv_tgt_qk = 0
            for mi in range(3):
                wsb = (wqsb, wksb, wvsb)[mi]
                dst = (qtsb, ktsb, vtsb)[mi]
                ev_prev = qk_prev_consumed if mi < 2 else v_prev_consumed
                ev_sem = s_sacc if mi < 2 else s_prd
                for dch in range(8):
                    bank = pQK[:, (dch % 2) * 512: (dch % 2) * 512 + 512]
                    ev_gate = qkv_bank_free[dch % 2]

                    def pe(te, wsb=wsb, dch=dch, bank=bank, par=par, x0T_tgt=x0T_tgt,
                           first=(mi == 0 and dch == 0), firstm=(dch == 0), ev_gate=ev_gate,
                           wsem=wq_sems[mi], wq_t=wq_tgts[mi], sb_=sb_, pv2g=pvb_track[2]):
                        if first:
                            te.wait_ge(s_x0T.h, x0T_tgt)
                            if pv2g > 0:
                                te.wait_ge(s_pev.h, pv2g)
                        if firstm and sb_ == 0:
                            te.wait_ge(wsem.h, wq_t)
                        if ev_gate > 0:
                            te.wait_ge(s_qkv.h, ev_gate)
                        for k in range(8):
                            mm = te.matmul(bank,
                                           wsb[:, (k * 8 + dch) * 128: (k * 8 + dch) * 128 + 128],
                                           x0T[:, par * 4096 + k * 512: par * 4096 + k * 512 + 512],
                                           start=(k == 0), stop=(k == 7))
                        mm.then_inc(s_peq.h, 1)
                    (QPE if mi < 2 else VPE).append(pe)
                    peq_t = s_peq.inc()

                    def ac(a, peq_t=peq_t, dst=dst, dch=dch, bank=bank,
                           ev_prev=ev_prev, ev_sem=ev_sem, first=(dch == 0)):
                        a.wait_ge(s_peq.h, peq_t)
                        if first and ev_prev > 0:
                            a.wait_ge(ev_sem.h, ev_prev)
                        a.activation(dst[:, dch * 512: dch * 512 + 512], bank,
                                     AF.Copy).then_inc(s_qkv.h, 1)
                    (QAC if mi < 2 else VAC).append(ac)
                    qkv_bank_free[dch % 2] = s_qkv.inc()
                if mi == 1:
                    qkv_tgt_qk = s_qkv.n
            peq_after_sb[sb_] = s_peq.n
            qkv_tgt = s_qkv.n

            if pend_pv is not None:
                for kind, f in pend_pv[0]:
                    P.pe.append(f)
                for tag, f in pend_pv[1]:
                    P.act.append(f)
                pend_pv = None
            for f in QPE:
                P.pe.append(f)
            for f in QAC:
                P.act.append(f)
            for f in VPE:
                P.pe.append(f)
            for f in VAC:
                P.act.append(f)

            prod_hist = {}
            for t in range(16):
                krbuf = t % 2
                ga = sacc_track[0]
                gb = sacc_track[1]
                kr_cov = prod_hist.get(t - 2, 0)

                def dvK(v, t=t, krbuf=krbuf, qkv_tgt=qkv_tgt_qk,
                        first=(t == 0), ga=ga, gb=gb, kr_cov=kr_cov):
                    if first:
                        v.wait_ge(s_qkv.h, qkv_tgt)
                    if kr_cov > 0:
                        v.wait_ge(s_prod.h, kr_cov)
                    src0 = (t % 2) * 64
                    ch = t // 2
                    v.tensor_copy(kr[0:64, krbuf * 512: krbuf * 512 + 512],
                                  ktsb[src0:src0 + 64, ch * 512: ch * 512 + 512])
                    v.tensor_copy(kr[64:128, krbuf * 512: krbuf * 512 + 512],
                                  ktsb[src0:src0 + 64, ch * 512: ch * 512 + 512]).then_inc(s_krc.h, 1)
                    v.wait_ge(s_krc.h, KRC[0] + 1)
                    KRC[0] += 1
                    krep = bass.AP(kr, krbuf * 512, [[1024, 128], [0, 4], [1, 512]])
                    if ga > 0:
                        v.wait_ge(s_sacc.h, ga)
                    v.tensor_tensor(
                        bass.AP(prodb, 0, [[4096, 128], [512, 4], [1, 512]]),
                        bass.AP(qtsb, 0, [[4096, 128], [512, 4], [1, 512]]),
                        krep, ALU.mult).then_inc(s_prod.h, 1)
                    if gb > 0:
                        v.wait_ge(s_sacc.h, gb)
                    v.tensor_tensor(
                        bass.AP(prodb, 2048, [[4096, 128], [512, 4], [1, 512]]),
                        bass.AP(qtsb, 2048, [[4096, 128], [512, 4], [1, 512]]),
                        krep, ALU.mult).then_inc(s_prod.h, 1)
                P.dve.append(dvK)
                prod_a = s_prod.inc()
                prod_b = s_prod.inc()
                prod_hist[t] = prod_b

                for c in range(8):
                    prod_t = prod_a if c < 4 else prod_b

                    def peS(te, t=t, c=c, prod_t=prod_t,
                            first=(t == 0 and c == 0), sgg=exp_gate_prev,
                            pvg=(0 if sb_ == 0 else pev_final_prev)):
                        te.wait_ge(s_prod.h, prod_t)
                        if first and sgg > 0:
                            te.wait_ge(s_exp2.h, sgg)
                        if first and pvg > 0:
                            te.wait_ge(s_pev.h, pvg)
                        mm = te.matmul(pS[32 * (c % 4): 32 * (c % 4) + 32,
                                          (c // 4) * 512: (c // 4) * 512 + 512],
                                       indt[:, t * 32: t * 32 + 32],
                                       prodb[:, c * 512: c * 512 + 512],
                                       start=(t == 0), stop=(t == 15),
                                       tile_position=(0, 32 * (c % 4)),
                                       skip_group_check=True)
                        mm.then_inc(s_sacc.h, 1)
                    P.pe.append(peS)
                    if c == 3:
                        sacc_track[0] = s_sacc.inc()
                    elif c == 7:
                        sacc_track[1] = s_sacc.inc()
                    else:
                        s_sacc.inc()
            sacc_final = s_sacc.n

            def acE(a, sacc_final=sacc_final, png=pnorm_gate_prev):
                a.wait_ge(s_sacc.h, sacc_final)
                if png > 0:
                    a.wait_ge(s_pnorm.h, png)
                a.activation(praw[:, 0:1024], pS[:, 0:1024], AF.Exp,
                             bias=expb[:, :], scale=EXP_SCALE).then_inc(s_exp2.h, 1)
            P.act.append(acE)
            exp_t = s_exp2.inc()
            exp_gate_prev = exp_t

            pfm_reuse = 0 if sb_ == 0 else pexp_final

            def peR(te, exp_t=exp_t, rg=(0 if sb_ == 0 else s_rcp2.n)):
                te.wait_ge(s_exp2.h, exp_t)
                if rg > 0:
                    te.wait_ge(s_rcp2.h, rg)
                te.matmul(pR[0:8, 0:512], rind[:, :], praw[:, 0:512], start=True, stop=True)
                te.matmul(pA[0:8, 0:512], rind[:, :], praw[:, 512:1024],
                          start=True, stop=True).then_inc(s_rsum.h, 1)
            P.pe.append(peR)
            rsum_t = s_rsum.inc()

            def acR(a, rsum_t=rsum_t):
                a.wait_ge(s_rsum.h, rsum_t)
                a.activation(lnb[:, :], pR[0:8, 0:512], AF.Ln)
                a.drain()
                a.activation(rcpf1[0:8, :], lnb[:, :], AF.Exp, scale=-1.0)
                a.drain()
                a.activation(lnb[:, :], pA[0:8, 0:512], AF.Ln)
                a.drain()
                a.activation(rcpf2[0:8, :], lnb[:, :], AF.Exp, scale=-1.0).then_inc(s_rcp2.h, 1)
            P.act.append(acR)
            rcp_t = s_rcp2.inc()

            def peX(te, rcp_t=rcp_t, pxg=pvb_track[1]):
                te.wait_ge(s_rcp2.h, rcp_t)
                if pxg > 0:
                    te.wait_ge(s_pev.h, pxg)
                te.matmul(pBIG[:, 1024:1536], e8[:, :], rcpf1[:, :], start=True, stop=True)
                te.matmul(pBIG[:, 1536:2048], e8[:, :], rcpf2[:, :],
                          start=True, stop=True).then_inc(s_rexp.h, 1)
            P.pe.append(peX)
            rexp_t = s_rexp.inc()

            def dvN(v, rexp_t=rexp_t, pfm_reuse=pfm_reuse):
                v.wait_ge(s_rexp.h, rexp_t)
                if pfm_reuse > 0:
                    v.wait_ge(s_pexp.h, pfm_reuse)
                v.tensor_tensor(pfm[:, 0:1024], pBIG[:, 1024:2048], praw[:, 0:1024],
                                ALU.mult).then_inc(s_pnorm.h, 1)
            P.dve.append(dvN)
            pnorm_t = s_pnorm.inc()
            pnorm_gate_prev = pnorm_t

            ECL = []
            RCL = []
            AVL = []
            ao_bank = {0: ao_gate_v, 1: ao_gate_v}
            for pi, (cp, k) in enumerate([(g * 2 + sub, kk)
                                          for g in range(4)
                                          for kk in range(8)
                                          for sub in range(2)]):
                pb = pi % 3
                slot = pi % 2
                grp = cp // 4
                exv = (expf4_0, expf4_1, expf4_2, expf4_3)[cp % 4]
                rbank = pA if cp % 2 == 0 else pR
                pvb = (pBIG[:, 0:1024], pBIG[:, 1024:2048], pQK[:, 0:1024])[pb]
                bank_gate = pvb_track[pb]
                qg = qkv_tgt if pi == 2 else 0

                def peE(te, pvb=pvb, exv=exv, grp=grp, k=k, pnorm_t=pnorm_t,
                        first=(pi == 0), bank_gate=bank_gate, qg=qg, e4t=ex4_tgt):
                    if first:
                        te.wait_ge(s_pnorm.h, pnorm_t)
                        te.wait_ge(s_ex4.h, e4t)
                    if qg > 0:
                        te.wait_ge(s_qkv.h, qg)
                    if bank_gate > 0:
                        te.wait_ge(s_pev.h, bank_gate)
                    for hh in range(2):
                        mm = te.matmul(pvb[:, hh * 512: hh * 512 + 512],
                                       exv[:, (hh * 8 + k) * 128: (hh * 8 + k) * 128 + 128],
                                       pfm[:, grp * 512: grp * 512 + 512],
                                       start=True, stop=True)
                    mm.then_inc(s_pexp.h, 1)
                ECL.append(peE)
                pexp_t = s_pexp.inc()

                ev_gate = sl_prd[slot]

                def acV(a, pvb=pvb, slot=slot, pexp_t=pexp_t, ev_gate=ev_gate):
                    a.wait_ge(s_pexp.h, pexp_t)
                    if ev_gate > 0:
                        a.wait_ge(s_prd.h, ev_gate)
                    a.activation(pexpsb[:, slot * 1024: slot * 1024 + 1024],
                                 pvb, AF.Copy).then_inc(s_pev.h, 1)
                AVL.append((pi, acV))
                pvb_track[pb] = s_pev.inc()

                prd_gate = sl_pacc[slot]
                pev_t = pvb_track[pb]

                def dvD(v, slot=slot, k=k, pev_t=pev_t, prd_gate=prd_gate,
                        vq=(qkv_tgt if pi == 0 else 0)):
                    v.wait_ge(s_pev.h, pev_t)
                    if vq > 0:
                        v.wait_ge(s_qkv.h, vq)
                    if prd_gate > 0:
                        v.wait_ge(s_pacc.h, prd_gate)
                    v.tensor_tensor(
                        bass.AP(prdsb, slot * 1024, [[2048, 128], [512, 2], [1, 512]]),
                        bass.AP(pexpsb, slot * 1024, [[2048, 128], [512, 2], [1, 512]]),
                        bass.AP(vtsb, k * 512, [[4096, 128], [0, 2], [1, 512]]),
                        ALU.mult).then_inc(s_prd.h, 1)
                P.dve.append(dvD)
                sl_prd[slot] = s_prd.inc()

                prd_t = sl_prd[slot]
                ag = ao_bank[cp % 2]

                def peA(te, slot=slot, k=k, rbank=rbank, prd_t=prd_t, ag=ag):
                    te.wait_ge(s_prd.h, prd_t)
                    if k == 0 and ag > 0:
                        te.wait_ge(s_aoev.h, ag)
                    for hh in range(2):
                        te.matmul(rbank[hh * 64: hh * 64 + 64, :],
                                  redda[:, 0:64],
                                  prdsb[:, slot * 1024 + hh * 512: slot * 1024 + hh * 512 + 512],
                                  start=(k == 0), stop=(k == 7),
                                  tile_position=(0, hh * 64),
                                  skip_group_check=True).then_inc(s_pacc.h, 1)
                RCL.append(peA)
                s_pacc.inc()
                sl_pacc[slot] = s_pacc.inc()

                if k == 7:
                    pacc_fin = s_pacc.n

                    def acA(a, cp=cp, sb_=sb_, rbank=rbank, pacc_fin=pacc_fin):
                        a.wait_ge(s_pacc.h, pacc_fin)
                        a.activation(attnoutT[:, cp * 2048 + sb_ * 512: cp * 2048 + sb_ * 512 + 512],
                                     rbank[:, :], AF.Copy).then_inc(s_aoev.h, 1)
                    AVL.append((pi, acA))
                    ao_bank[cp % 2] = s_aoev.inc()
            ao_gate_v = max(ao_bank.values())
            DEPTH = 3
            pe_pipe = []
            for p in range(64 + DEPTH):
                if p < 64:
                    pe_pipe.append(('E', ECL[p]))
                if p >= DEPTH:
                    pe_pipe.append(('R', RCL[p - DEPTH]))
            pend_pv = (pe_pipe, AVL)
            pexp_final = s_pexp.n
            prd_final = s_prd.n
            pev_final_prev = s_pev.n
            aoev_prev = s_aoev.n
        if pend_pv is not None:
            for kind, f in pend_pv[0]:
                P.pe.append(f)
            for tag, f in pend_pv[1]:
                P.act.append(f)
            pend_pv = None
        ao_done = s_aoev.n

        def gp_wfc(g, pf=peq_after_sb[NSB - 1]):
            g.wait_ge(s_peq.h, pf)
            g.dma_start(wfcsb[:, :], bass.AP(dwfc, 0, [[1024, 128], [1024 * 128, 8], [1, 1024]])).then_inc(s_wf2.h, 16)
        P.gp.append(gp_wfc)
        wfc_tgt = s_wf2.inc(16)

        fc_bank_free = {0: 0, 1: 0, 2: 0, 3: 0}
        x1_final = 0
        gcs_t = 0
        x0f_gate_prev = 0
        peq_final_b = s_peq.n
        pend_dv4 = [None]

        def emit_dv4(gcg=0, P=P):
            fn, tt_p = pend_dv4[0]
            if gcg > 0:
                def wrapped(v, fn=fn, gcg=gcg):
                    v.wait_ge(s_gcs.h, gcg)
                    fn(v)
                P.dve.append(wrapped)
            else:
                P.dve.append(fn)
            x1_t = s_x1.inc()

            def pe4(te, x1_t=x1_t, tt=tt_p, rcp_gate=s_rcp2.n):
                te.wait_ge(s_x1.h, x1_t)
                if tt == 0:
                    te.wait_ge(s_rcp2.h, rcp_gate)
                m1 = te.matmul(pR[0:1, 0:512], onesb[:, 0:1], x1bfgc[:, 0:512],
                               start=(tt == 0), stop=(tt == NTT - 1))
                m2 = te.matmul(pA[0:1, 0:512], onesb[:, 0:1], x1bfgc[:, 512:1024],
                               start=(tt == 0), stop=(tt == NTT - 1))
                m2.then_inc(s_gcs.h, 1)
            P.pe.append(pe4)
            s_gcs.inc()
            pend_dv4[0] = None
            return x1_t

        for tt in range(NTT):
            par = tt % 2
            stile = sumtile if par == 0 else sumtile2
            gate = 0 if tt < 2 else x0f_gate_prev
            fsem = x0f_sems[tt % 2]

            def sp2(s, tt=tt, b=b, gate=gate, fsem=fsem, pqf=peq_final_b):
                if tt < 2:
                    s.wait_ge(s_peq.h, pqf)
                if gate > 0:
                    s.wait_ge(s_sum.h, gate)
                s.dma_start(x0f32[:, (tt % 2) * 1024: (tt % 2) * 1024 + 1024],
                            dx.ap()[b * S + tt * 128: b * S + (tt + 1) * 128, :]).then_inc(fsem.h, 16)
            P.sp.append(sp2)
            x0f_t = fsem.inc(16)

            slot0 = (tt % 2) * 2
            sum_t = 0
            for half in range(2):
                slot = slot0 + half
                bfree = fc_bank_free[slot]

                def pe3(te, tt=tt, half=half, slot=slot, ao_done=ao_done, bfree=bfree,
                        first=(tt == 0 and half == 0), wfc_tgt=wfc_tgt):
                    if first:
                        te.wait_ge(s_aoev.h, ao_done)
                        te.wait_ge(s_wf2.h, wfc_tgt)
                        te.wait_ge(s_winit.h, N_WINIT)
                    if bfree > 0:
                        te.wait_ge(s_sum.h, bfree)
                    for c in range(8):
                        mm = te.matmul(SLOT[slot],
                                       attnoutT[:, c * 2048 + tt * 128: c * 2048 + tt * 128 + 128],
                                       wfcsb[:, c * 1024 + half * 512: c * 1024 + half * 512 + 512],
                                       start=(c == 0), stop=(c == 7))
                    mm.then_inc(s_fc.h, 1)
                P.pe.append(pe3)
                fc_t = s_fc.inc()

                def dv2(v, fc_t=fc_t, x0f_t=x0f_t, half=half, tt=tt, slot=slot,
                        fsem=fsem, stile=stile):
                    v.wait_ge(s_fc.h, fc_t)
                    v.wait_ge(fsem.h, x0f_t)
                    v.tensor_tensor(stile[:, half * 512: half * 512 + 512],
                                    SLOT[slot],
                                    x0f32[:, (tt % 2) * 1024 + half * 512: (tt % 2) * 1024 + half * 512 + 512],
                                    ALU.add).then_inc(s_sum.h, 1)
                P.dve.append(dv2)
                sum_t = s_sum.inc()
                fc_bank_free[slot] = sum_t
            x0f_gate_prev = sum_t

            def dv3(v, sum_t=sum_t, par=par, stile=stile):
                v.wait_ge(s_sum.h, sum_t)
                v.drain()
                v.bn_stats(stats[:, 0:6], stile[:, 0:512])
                v.bn_stats(stats[:, 6:12], stile[:, 512:1024])
                v.drain()
                v.bn_aggr((aggr if par == 0 else aggr2)[:, :], stats[:, :]).then_inc(s_st.h, 1)
            P.dve.append(dv3)
            st_t = s_st.inc()

            def ac3(a, st_t=st_t, par=par):
                a.wait_ge(s_st.h, st_t)
                a.activation(lncol[:, par * 2: par * 2 + 1],
                             (aggr if par == 0 else aggr2)[:, 1:2],
                             AF.Ln, bias=epscol[:, :])
                a.drain()
                a.activation(lncol[:, par * 2 + 1: par * 2 + 2], lncol[:, par * 2: par * 2 + 1],
                             AF.Exp, scale=-0.5).then_inc(s_rstd.h, 1)
            P.act.append(ac3)
            rstd_t = s_rstd.inc()

            if pend_dv4[0] is not None:
                x1_final = emit_dv4()

            def dv4(v, rstd_t=rstd_t, tt=tt, par=par, stile=stile):
                v.wait_ge(s_rstd.h, rstd_t)
                ag = aggr if par == 0 else aggr2
                v.tensor_scalar(x1[:, tt * 1024: tt * 1024 + 1024], stile[:, :],
                                ag[:, 0:1], lncol[:, par * 2 + 1: par * 2 + 2],
                                ALU.subtract, ALU.mult)
                v.tensor_scalar(x1bfgc[:, :], stile[:, :],
                                ag[:, 0:1], lncol[:, par * 2 + 1: par * 2 + 2],
                                ALU.subtract, ALU.mult)
                v.drain()
                nc.vector.engine_nop().then_inc(s_x1.h, 1)
            pend_dv4[0] = (dv4, tt)
        x1_final = emit_dv4(gcg=s_gcs.n)
        gcs_t = s_gcs.n

        def gp_gw(g, x1_f=x1_final, gcs_f=s_gcs.n):
            g.wait_ge(s_x1.h, x1_f)
            g.wait_ge(s_gcs.h, gcs_f)
            g.dma_start(gw1sb[:, :], bass.AP(dgw1, 0, [[512, 128], [512 * 128, 8], [1, 512]])).then_inc(s_wgw.h, 16)
            g.dma_start(gw2sb[:, :], bass.AP(dgw2, 0, [[256, 128], [256 * 128, 4], [1, 256]])).then_inc(s_wgw.h, 16)
        P.gp.append(gp_gw)
        gw_tgt = s_wgw.inc(16 * 2)

        if b == 0 and DEBUG[0]:
            def sp_dbg(s, x1f=x1_final, aof=ao_done):
                s.wait_ge(s_x1.h, x1f)
                s.wait_ge(s_aoev.h, aof)
                for tt in range(NTT):
                    s.dma_start(dbg_x1.ap()[tt * 128:(tt + 1) * 128, :],
                                x1[:, tt * 1024: tt * 1024 + 1024]).then_inc(s_dmp.h, 16)
                s.dma_start(dbg_ao.ap()[:, :], attnoutT[:, :]).then_inc(s_dmp.h, 16)
            P.sp.append(sp_dbg)
            dump_tgt = s_dmp.inc(16 * 17)
        else:
            dump_tgt = 0

        fc_final_c = s_fc.n
        l_prev_seq = s_l.n
        sum_final_c = s_sum.n
        l_bank_free = {0: 0, 1: 0, 2: 0, 3: 0}
        l2_bank_free = {0: 0, 1: 0}
        h1_gate = {0: 0, 1: 0}
        h2_gate = {0: 0, 1: 0}
        h3_gate = {0: 0, 1: 0}
        shad_ready_h = {0: 0, 1: 0}
        shad_base_h = {0: 0, 1: 0}
        cast_ready_h = {0: 0, 1: 0}
        l1_after = {0: 0, 1: 0}
        tf_free = [0]
        epi_free = [0]
        c4n_free = [0]
        outT_free = [0, 0]
        stile_free = {0: 0, 1: 0}
        OBGG = [None]
        dmp272 = 272 if (b == 0 and DEBUG[0]) else 0

        def emit_init(T, hf):
            shad_base_h[hf] = shad_sems[hf].n
            for sti in range(2):
                st = 2 * hf + sti
                og = OBGG[0]
                lg = l_prev_seq if T == 0 else l1_after[hf]

                def dvI(v, T=T, st=st, og=og, x1f=x1_final,
                        wqt=(wq_tgts if T == 0 else None)):
                    v.wait_ge(s_x1.h, x1f)
                    if wqt is not None:
                        v.wait_ge(s_wq.h, wqt[0])
                        v.wait_ge(s_wk.h, wqt[1])
                        v.wait_ge(s_wv.h, wqt[2])
                    if og is not None:
                        v.wait_ge(og[0].h, og[1])
                    v.tensor_copy(xstg[:, :],
                                  x1[:, (T * 4 + st) * 1024: (T * 4 + st) * 1024 + 1024]).then_inc(s_tw.h, 1)
                P.dve.append(dvI)
                ct = s_tw.inc()

                def spI(s, st=st, ct=ct, lg=lg, hsem=shad_sems[hf],
                        fcf=(fc_final_c if T == 0 else 0),
                        dmp=(dmp272 if T == 0 else 0)):
                    s.wait_ge(s_tw.h, ct)
                    if fcf > 0:
                        s.wait_ge(s_fc.h, fcf)
                    if dmp > 0:
                        s.wait_ge(s_dmp.h, dmp)
                    if lg > 0:
                        s.wait_ge(s_l.h, lg)
                    for c in range(8):
                        s.dma_start(outTbf[:, c * 512 + st * 128: c * 512 + st * 128 + 128],
                                    xstg[:, c * 128: c * 128 + 128],
                                    transpose=True).then_inc(hsem.h, 16)
                P.sp.append(spI)
                OBGG[0] = (shad_sems[hf], shad_sems[hf].inc(16 * 8))
            shad_ready_h[hf] = shad_sems[hf].n

        def emit_widen(T, hf):
            par = T % 2
            oT = (outT0, outT1)[par]
            sr = shad_ready_h[hf]
            ofree = outT_free[par]

            def dvW(v, oT=oT, hf=hf, sr=sr, hsem=shad_sems[hf], ofree=ofree,
                    gcg=(gcnet_done if (T == 0 and hf == 0) else 0)):
                v.wait_ge(hsem.h, sr)
                if gcg > 0:
                    v.wait_ge(s_gc.h, gcg)
                if ofree > 0:
                    v.wait_ge(s_aoev.h, ofree)
                v.tensor_copy(
                    bass.AP(oT, hf * 256, [[4096, 128], [512, 8], [1, 256]]),
                    bass.AP(outTbf, hf * 256, [[4096, 128], [512, 8], [1, 256]]))
                v.drain()
                nc.vector.engine_nop().then_inc(s_tw.h, 1)
            P.dve.append(dvW)
            return s_tw.inc()

        for hf_ in range(2):
            emit_init(0, hf_)

        def emit_gc(fn, engine):
            getattr(P, engine).append(fn)
            return s_gc.inc()

        g_t = emit_gc(lambda v, gcs_t=gcs_t, gw_tgt=gw_tgt, dt_=dump_tgt: (
            v.wait_ge(s_gcs.h, gcs_t),
            v.wait_ge(s_wgw.h, gw_tgt),
            (v.wait_ge(s_dmp.h, dt_) if dt_ > 0 else None),
            v.tensor_copy(growsf[0:1, 0:512], pR[0:1, 0:512]),
            v.tensor_copy(growsf[0:1, 512:1024], pA[0:1, 0:512]),
            v.drain(),
            v.tensor_scalar(gcrow[0:1, :], growsf[0:1, :], 1.0 / S, None, ALU.mult),
            v.drain(),
            nc.vector.engine_nop().then_inc(s_gc.h, 1),
        ), "dve")

        def pe5(te, g_t=g_t):
            te.wait_ge(s_gc.h, g_t)
            for c in range(8):
                mm = te.matmul(pR[:, c:c + 1], gcrow[0:1, c * 128: c * 128 + 128],
                               onesb[0:1, 0:1], start=True, stop=True,
                               skip_group_check=True)
            mm.then_inc(s_gc.h, 1)
        g_t = emit_gc(pe5, "pe")

        g_t = emit_gc(lambda v, g_t=g_t: (
            v.wait_ge(s_gc.h, g_t),
            v.tensor_copy(gcT[:, :], pR[:, 0:8]),
            v.drain(),
            nc.vector.engine_nop().then_inc(s_gc.h, 1),
        ), "dve")

        def pe6(te, g_t=g_t):
            te.wait_ge(s_gc.h, g_t)
            for c in range(8):
                mm = te.matmul(pA[0:1, 0:512], gcT[:, c:c + 1],
                               gw1sb[:, c * 512: c * 512 + 512], start=(c == 0), stop=(c == 7),
                               skip_group_check=True)
            mm.then_inc(s_gc.h, 1)
        g_t = emit_gc(pe6, "pe")

        g_t = emit_gc(lambda v, g_t=g_t: (
            v.wait_ge(s_gc.h, g_t),
            v.tensor_tensor(g1pre[0:1, :], pA[0:1, 0:512], gb1row[0:1, :], ALU.add),
            v.drain(),
            nc.vector.engine_nop().then_inc(s_gc.h, 1),
        ), "dve")

        g_t = emit_gc(lambda a, g_t=g_t: (
            a.wait_ge(s_gc.h, g_t),
            a.activation(g1b[0:1, :], g1pre[0:1, :], AF.Gelu).then_inc(s_gc.h, 1),
        ), "act")

        def pe7(te, g_t=g_t):
            te.wait_ge(s_gc.h, g_t)
            for c in range(4):
                mm = te.matmul(pR[:, 16 + c: 17 + c], g1b[0:1, c * 128: c * 128 + 128],
                               onesb[0:1, 0:1], start=True, stop=True,
                               skip_group_check=True)
            mm.then_inc(s_gc.h, 1)
        g_t = emit_gc(pe7, "pe")

        g_t = emit_gc(lambda v, g_t=g_t: (
            v.wait_ge(s_gc.h, g_t),
            v.tensor_copy(g1T[:, :], pR[:, 16:20]),
            v.drain(),
            nc.vector.engine_nop().then_inc(s_gc.h, 1),
        ), "dve")

        def pe8(te, g_t=g_t):
            te.wait_ge(s_gc.h, g_t)
            for c in range(4):
                mm = te.matmul(pR[0:1, 256:512], g1T[:, c:c + 1],
                               gw2sb[:, c * 256: c * 256 + 256], start=(c == 0), stop=(c == 3),
                               skip_group_check=True)
            mm.then_inc(s_gc.h, 1)
        g_t = emit_gc(pe8, "pe")

        g_t = emit_gc(lambda v, g_t=g_t: (
            v.wait_ge(s_gc.h, g_t),
            v.tensor_tensor(g2pre[0:1, :], pR[0:1, 256:512], gb2row[0:1, :], ALU.add),
            v.drain(),
            nc.vector.engine_nop().then_inc(s_gc.h, 1),
        ), "dve")

        g_t = emit_gc(lambda a, g_t=g_t: (
            a.wait_ge(s_gc.h, g_t),
            a.activation(g2b[0:1, :], g2pre[0:1, :], AF.Gelu).then_inc(s_gc.h, 1),
        ), "act")

        def pe9(te, g_t=g_t):
            te.wait_ge(s_gc.h, g_t)
            for c in range(2):
                mm = te.matmul(pR[:, 24 + c: 25 + c], g2b[0:1, c * 128: c * 128 + 128],
                               onesb[0:1, 0:1], start=True, stop=True,
                               skip_group_check=True)
            mm.then_inc(s_gc.h, 1)
        g_t = emit_gc(pe9, "pe")

        g_t = emit_gc(lambda v, g_t=g_t: (
            v.wait_ge(s_gc.h, g_t),
            v.tensor_copy(g2T[:, :], pR[:, 24:26]),
            v.drain(),
            nc.vector.engine_nop().then_inc(s_gc.h, 1),
        ), "dve")

        def pe10(te, g_t=g_t):
            te.wait_ge(s_gc.h, g_t)
            for c in range(2):
                mm = te.matmul(pR[0:1, 30:31], g2T[:, c:c + 1], gw3sb[:, c:c + 1],
                               start=(c == 0), stop=(c == 1),
                               skip_group_check=True)
            mm.then_inc(s_gc.h, 1)
        g_t = emit_gc(pe10, "pe")

        g_t = emit_gc(lambda v, g_t=g_t: (
            v.wait_ge(s_gc.h, g_t),
            v.tensor_tensor(g3pre[0:1, :], pR[0:1, 30:31], arow[0:1, 2:3], ALU.add),
            v.drain(),
            nc.vector.engine_nop().then_inc(s_gc.h, 1),
        ), "dve")

        g_t = emit_gc(lambda a, g_t=g_t: (
            a.wait_ge(s_gc.h, g_t),
            a.activation(gtanh[0:1, :], g3pre[0:1, :], AF.Tanh, scale=0.5).then_inc(s_gc.h, 1),
        ), "act")

        def dv12(v, g_t=g_t):
            v.wait_ge(s_gc.h, g_t)
            v.tensor_scalar(arow[0:1, 3:4], gtanh[0:1, :], 0.5, 0.5, ALU.mult, ALU.add)
            v.drain()
            v.tensor_tensor(arow[0:1, 4:5], arow[0:1, 3:4], arow[0:1, 0:1], ALU.mult)
            v.drain()
            v.tensor_scalar(arow[0:1, 4:5], arow[0:1, 4:5], 1.4, 0.1, ALU.mult, ALU.add)
            v.drain()
            v.tensor_tensor(arow[0:1, 4:5], arow[0:1, 4:5], arow[0:1, 0:1], ALU.subtract)
            v.tensor_scalar(arow[0:1, 5:6], arow[0:1, 0:1], 0.3, None, ALU.mult)
            v.drain()
            v.tensor_tensor(abrow[0:1, 0:1], arow[0:1, 4:5], arow[0:1, 5:6], ALU.add)
            v.tensor_copy(abrow[0:1, 1:2], arow[0:1, 5:6])
            v.tensor_scalar(abrow[0:1, 2:3], arow[0:1, 1:2], 0.5, None, ALU.mult)
            v.drain()
            v.tensor_scalar(abrow[0:1, 3:4], abrow[0:1, 0:1], -1.0, None, ALU.mult)
            v.tensor_scalar(lnb[0:1, 0:128], onesf[0:1, :], abrow[0:1, 1:2], None, ALU.mult)
            v.drain()
            nc.vector.engine_nop().then_inc(s_gc.h, 1)
        g_t = emit_gc(dv12, "dve")

        def pe11(te, g_t=g_t):
            te.wait_ge(s_gc.h, g_t)
            te.matmul(pR[:, 28:30], onesf[0:1, :], abrow[0:1, 0:2], start=True, stop=True,
                      skip_group_check=True)
            te.matmul(pR[:, 30:32], onesf[0:1, :], abrow[0:1, 2:4],
                      start=True, stop=True,
                      skip_group_check=True).then_inc(s_gc.h, 1)
        g_t = emit_gc(pe11, "pe")

        g_t = emit_gc(lambda v, g_t=g_t: (
            v.wait_ge(s_gc.h, g_t),
            v.tensor_copy(ccols[:, 0:2], pR[:, 28:30]),
            v.tensor_copy(ccols[:, 2:4], pR[:, 30:32]),
            v.drain(),
            nc.vector.engine_nop().then_inc(s_gc.h, 1),
        ), "dve")
        gcnet_done = g_t

        widen_pend = {}
        for hf_ in range(2):
            widen_pend[hf_] = True

        for T in range(NFT):
            par_T = T % 2
            oT = (outT0, outT1)[par_T]
            if T == 0:
                for hf_ in range(2):
                    emit_widen(0, hf_)
            for cyc in range(NCY):
                for hf in range(2):
                    for dch in range(8):
                        bank = 2 * hf + dch % 2
                        bfree = l_bank_free[bank]

                        def peL1(te, dch=dch, hf=hf, bank=bank, bfree=bfree,
                                 sr=shad_ready_h[hf], sb0=shad_base_h[hf],
                                 hsem=shad_sems[hf],
                                 cr=cast_ready_h[hf], usecast=(cyc > 0),
                                 first=(dch == 0),
                                 needsum=(T == 0 and cyc == 0), sum_f=sum_final_c):
                            if first:
                                if usecast:
                                    te.wait_ge(s_tw.h, cr)
                                else:
                                    te.wait_ge(hsem.h, sr)
                                if needsum:
                                    te.wait_ge(s_sum.h, sum_f)
                            if bfree > 0:
                                te.wait_ge(s_h.h, bfree)
                            for k in range(8):
                                mm = te.matmul(SLOT[bank][:, 0:256],
                                               cw1sb[:, (k * 8 + dch) * 128: (k * 8 + dch) * 128 + 128],
                                               outTbf[:, k * 512 + hf * 256: k * 512 + hf * 256 + 256],
                                               start=(k == 0), stop=(k == 7))
                            mm.then_inc(s_l.h, 1)
                        P.pe.append(peL1)
                        l_t = s_l.inc()
                        hg = h1_gate[hf]

                        def acL1(a, l_t=l_t, dch=dch, hf=hf, bank=bank, hg=hg):
                            a.wait_ge(s_l.h, l_t)
                            if hg > 0:
                                a.wait_ge(s_l.h, hg)
                            a.activation(h1T[:, dch * 512 + hf * 256: dch * 512 + hf * 256 + 256],
                                         SLOT[bank][:, 0:256], AF.Gelu,
                                         bias=cb1sb[:, dch:dch + 1]).then_inc(s_h.h, 1)
                        P.act.append(acL1)
                        l_bank_free[bank] = s_h.inc()
                    l1_after[hf] = s_l.n
                    h1_done = s_h.n
                    if cyc == NCY - 1 and T + 1 < NFT:
                        emit_init(T + 1, hf)

                    for dch in range(4):
                        q = dch % 2
                        bfree = l2_bank_free[q]

                        def peL2(te, dch=dch, hf=hf, q=q, h1_done=h1_done, bfree=bfree):
                            if dch == 0:
                                te.wait_ge(s_h.h, h1_done)
                            if bfree > 0:
                                te.wait_ge(s_h.h, bfree)
                            for k in range(8):
                                mm = te.matmul(pQK[:, q * 512: q * 512 + 256],
                                               cw2sb[:, (k * 4 + dch) * 128: (k * 4 + dch) * 128 + 128],
                                               h1T[:, k * 512 + hf * 256: k * 512 + hf * 256 + 256],
                                               start=(k == 0), stop=(k == 7))
                            mm.then_inc(s_l.h, 1)
                        P.pe.append(peL2)
                        l_t = s_l.inc()
                        hg = h2_gate[hf]

                        def acL2(a, l_t=l_t, dch=dch, hf=hf, q=q, hg=hg):
                            a.wait_ge(s_l.h, l_t)
                            if hg > 0:
                                a.wait_ge(s_l.h, hg)
                            a.activation(h2T[:, dch * 512 + hf * 256: dch * 512 + hf * 256 + 256],
                                         pQK[:, q * 512: q * 512 + 256],
                                         AF.Gelu, bias=cb2sb[:, dch:dch + 1]).then_inc(s_h.h, 1)
                        P.act.append(acL2)
                        l2_bank_free[q] = s_h.inc()
                    h1_gate[hf] = s_l.n
                    h2_done = s_h.n

                    for dch in range(2):
                        bank = 2 * hf + dch
                        bfree = l_bank_free[bank]

                        def peL3(te, dch=dch, hf=hf, bank=bank, h2_done=h2_done, bfree=bfree):
                            if dch == 0:
                                te.wait_ge(s_h.h, h2_done)
                            if bfree > 0:
                                te.wait_ge(s_h.h, bfree)
                            for k in range(4):
                                mm = te.matmul(SLOT[bank][:, 0:256],
                                               cw3sb[:, (k * 2 + dch) * 128: (k * 2 + dch) * 128 + 128],
                                               h2T[:, k * 512 + hf * 256: k * 512 + hf * 256 + 256],
                                               start=(k == 0), stop=(k == 3))
                            mm.then_inc(s_l.h, 1)
                        P.pe.append(peL3)
                        l_t = s_l.inc()
                        hg = h3_gate[hf]

                        def acL3(a, l_t=l_t, dch=dch, hf=hf, bank=bank, hg=hg):
                            a.wait_ge(s_l.h, l_t)
                            if hg > 0:
                                a.wait_ge(s_l.h, hg)
                            a.activation(h3T[:, dch * 512 + hf * 256: dch * 512 + hf * 256 + 256],
                                         SLOT[bank][:, 0:256], AF.Gelu,
                                         bias=cb3sb[:, dch:dch + 1]).then_inc(s_h.h, 1)
                        P.act.append(acL3)
                        l_bank_free[bank] = s_h.inc()
                    h3_done = s_h.n

                    def peT(te, hf=hf, h3_done=h3_done, tfree=tf_free[0],
                            efree=epi_free[0], needgc=(T == 0 and cyc == 0), g_t=gcnet_done):
                        te.wait_ge(s_h.h, h3_done)
                        if needgc:
                            te.wait_ge(s_gc.h, g_t)
                        if tfree > 0:
                            te.wait_ge(s_h.h, tfree)
                        if efree > 0:
                            te.wait_ge(s_aoev.h, efree)
                        for k in range(2):
                            mm = te.matmul(pR[0:1, 0:256], cw4sb[:, k:k + 1],
                                           h3T[:, k * 512 + hf * 256: k * 512 + hf * 256 + 256],
                                           start=(k == 0), stop=(k == 1),
                                           skip_group_check=True)
                        mm.then_inc(s_l.h, 1)
                    P.pe.append(peT)
                    h3_gate[hf] = s_l.n
                    tf_t = s_l.inc()

                    def acT(a, tf_t=tf_t):
                        a.wait_ge(s_l.h, tf_t)
                        a.activation(tanhrow[0:1, :], pR[0:1, 0:256], AF.Tanh,
                                     scale=0.5, bias=abrow[0:1, 2:3]).then_inc(s_h.h, 1)
                    P.act.append(acT)
                    th_t = s_h.inc()

                    def peB(te, th_t=th_t):
                        te.wait_ge(s_h.h, th_t)
                        te.matmul(pA[:, 0:256], lnb[0:1, 0:128], tanhrow[0:1, :],
                                  start=True, stop=True,
                                  skip_group_check=True).then_inc(s_l.h, 1)
                    P.pe.append(peB)
                    bc_t = s_l.inc()

                    cg = c4n_free[0]

                    def acC4(a, bc_t=bc_t, cg=cg):
                        a.wait_ge(s_l.h, bc_t)
                        if cg > 0:
                            a.wait_ge(s_tw.h, cg)
                        a.activation(C4N[:, 256:512], pA[:, 0:256], AF.Identity,
                                     bias=ccols[:, 0:1])
                        a.activation(bass.AP(C4N, 0, [[768, 128], [512, 2], [1, 256]]),
                                     bass.AP(pA, 0, [[512, 128], [0, 2], [1, 256]]),
                                     AF.Identity, scale=-1.0,
                                     bias=ccols[:, 3:4]).then_inc(s_h.h, 1)
                    P.act.append(acC4)
                    tf_free[0] = s_h.inc()
                    c4t = tf_free[0]

                    o = hf * 256

                    def dvP(v, o=o, oT=oT, c4t=c4t):
                        v.wait_ge(s_h.h, c4t)
                        for g in range(4):
                            v.tensor_tensor(
                                bass.AP(Ptw, g * 512, [[2048, 128], [256, 2], [1, 256]]),
                                bass.AP(oT, g * 512 + o, [[4096, 128], [2048, 2], [1, 256]]),
                                bass.AP(C4N, 0 if g < 2 else 256,
                                        [[768, 128], [256, 2], [1, 256]]),
                                ALU.mult)
                        v.drain()
                        nc.vector.engine_nop().then_inc(s_tw.h, 1)
                    P.dve.append(dvP)
                    c4n_free[0] = s_tw.inc()

                    def dvU(v, o=o, oT=oT, hf=hf, docast=(cyc < NCY - 1)):
                        v.tensor_tensor(
                            bass.AP(oT, o, [[4096, 128], [512, 2], [1, 256]]),
                            bass.AP(Ptw, 256, [[2048, 128], [512, 2], [1, 256]]),
                            bass.AP(oT, o, [[4096, 128], [512, 2], [1, 256]]),
                            ALU.add)
                        v.tensor_tensor(
                            bass.AP(oT, 2048 + o, [[4096, 128], [512, 2], [1, 256]]),
                            bass.AP(Ptw, 0, [[2048, 128], [512, 2], [1, 256]]),
                            bass.AP(oT, 2048 + o, [[4096, 128], [512, 2], [1, 256]]),
                            ALU.add)
                        v.tensor_tensor(
                            bass.AP(oT, 1024 + o, [[4096, 128], [512, 2], [1, 256]]),
                            bass.AP(Ptw, 1280, [[2048, 128], [512, 2], [1, 256]]),
                            bass.AP(oT, 1024 + o, [[4096, 128], [512, 2], [1, 256]]),
                            ALU.add)
                        v.tensor_tensor(
                            bass.AP(oT, 3072 + o, [[4096, 128], [512, 2], [1, 256]]),
                            bass.AP(Ptw, 1024, [[2048, 128], [512, 2], [1, 256]]),
                            bass.AP(oT, 3072 + o, [[4096, 128], [512, 2], [1, 256]]),
                            ALU.add)
                        if docast:
                            v.drain()
                            v.tensor_copy(
                                bass.AP(outTbf, o, [[4096, 128], [512, 8], [1, 256]]),
                                bass.AP(oT, o, [[4096, 128], [512, 8], [1, 256]])).then_inc(s_tw.h, 1)
                        else:
                            v.drain()
                            nc.vector.engine_nop().then_inc(s_tw.h, 1)
                    P.dve.append(dvU)
                    cast_ready_h[hf] = s_tw.inc()

            x1base = T * 4096
            if T + 1 < NFT:
                emit_widen(T + 1, 0)
            ep_first = True
            ep_bank = [0]
            for st in range(4):
                par = st % 2
                stile = sumtile if par == 0 else sumtile2
                gtok = T * 4 + st

                def peU(te, st=st, oT=oT, ep_first=ep_first, tfree=tf_free[0],
                        ebank=ep_bank[0],
                        twA=cast_ready_h[0], twB=cast_ready_h[1]):
                    if ep_first:
                        te.wait_ge(s_h.h, tfree)
                        te.wait_ge(s_tw.h, twA)
                        te.wait_ge(s_tw.h, twB)
                    if ebank > 0:
                        te.wait_ge(s_aoev.h, ebank)
                    for c in range(8):
                        mm = te.matmul((pA if c < 4 else pR)[:, (c % 4) * 128: (c % 4) * 128 + 128],
                                       oT[:, c * 512 + st * 128: c * 512 + st * 128 + 128],
                                       idf32[:, :], is_transpose=True,
                                       start=True, stop=True,
                                       skip_group_check=True)
                    mm.then_inc(s_l.h, 1)
                P.pe.append(peU)
                ep_first = False
                u_t = s_l.inc()

                def acU(a, u_t=u_t, stile=stile, sfree=stile_free[par],
                        gcg=(gcnet_done if (T == 0 and st == 0) else 0)):
                    a.wait_ge(s_l.h, u_t)
                    if gcg > 0:
                        a.wait_ge(s_gc.h, gcg)
                    if sfree > 0:
                        a.wait_ge(s_st.h, sfree)
                    a.activation(stile[:, 0:512], pA[:, :], AF.Copy)
                    a.activation(stile[:, 512:1024], pR[:, :],
                                 AF.Copy).then_inc(s_aoev.h, 1)
                P.act.append(acU)
                ue_t = s_aoev.inc()
                ep_bank[0] = ue_t
                epi_free[0] = ue_t
                outT_free[par_T] = ue_t

                def dvZ(v, ue_t=ue_t, stile=stile, base=x1base + st * 1024, par=par):
                    v.wait_ge(s_aoev.h, ue_t)
                    v.tensor_tensor(stile[:, :], x1[:, base: base + 1024],
                                    stile[:, :], ALU.add)
                    v.drain()
                    v.bn_stats(stats[:, 0:6], stile[:, 0:512])
                    v.bn_stats(stats[:, 6:12], stile[:, 512:1024])
                    v.drain()
                    v.bn_aggr((aggr if par == 0 else aggr2)[:, :], stats[:, :]).then_inc(s_st.h, 1)
                P.dve.append(dvZ)
                st_t = s_st.inc()
                stile_free[par] = st_t

                def acZ(a, st_t=st_t, par=par):
                    a.wait_ge(s_st.h, st_t)
                    a.activation(lncol[:, par * 2: par * 2 + 1],
                                 (aggr if par == 0 else aggr2)[:, 1:2],
                                 AF.Ln, bias=epscol[:, :])
                    a.drain()
                    a.activation(lncol[:, par * 2 + 1: par * 2 + 2],
                                 lncol[:, par * 2: par * 2 + 1],
                                 AF.Exp, scale=-0.5).then_inc(s_rstd.h, 1)
                P.act.append(acZ)
                rstd_t = s_rstd.inc()

                osem = od_sems[gtok % 2]
                x2f = x2_free[gtok % 2]

                def dvX2(v, rstd_t=rstd_t, gtok=gtok, x2f=x2f, osem=osem,
                         stile=stile, par=par):
                    v.wait_ge(s_rstd.h, rstd_t)
                    if x2f > 0:
                        v.wait_ge(osem.h, x2f)
                    v.tensor_scalar(x2slab[:, (gtok % 2) * 1024: (gtok % 2) * 1024 + 1024],
                                    stile[:, :],
                                    (aggr if par == 0 else aggr2)[:, 0:1],
                                    lncol[:, par * 2 + 1: par * 2 + 2],
                                    ALU.subtract, ALU.mult).then_inc(s_x2.h, 1)
                P.dve.append(dvX2)
                x2_t = s_x2.inc()

                def spO(s, x2_t=x2_t, gtok=gtok, b=b, osem=osem):
                    s.wait_ge(s_x2.h, x2_t)
                    s.dma_start(dout.ap()[b * S + gtok * 128: b * S + (gtok + 1) * 128, :],
                                x2slab[:, (gtok % 2) * 1024: (gtok % 2) * 1024 + 1024]).then_inc(osem.h, 16)
                P.sp.append(spO)
                od_t = osem.inc(16)
                x2_free[gtok % 2] = od_t
                if st == 1 and T + 1 < NFT:
                    emit_widen(T + 1, 1)

    P = new_block()
    odA, odB = s_od.n, s_odB.n

    def sp_final(s, odA=odA, odB=odB):
        s.wait_ge(s_od.h, odA)
        s.wait_ge(s_odB.h, odB)
    P.sp.append(sp_final)

    for prog in blocks:
        with nc.Block() as block:
            if prog.pe:
                @block.tensor
                def _(te, prog=prog):
                    for f in prog.pe:
                        f(te)
            if prog.act:
                @block.scalar
                def _(a, prog=prog):
                    for f in prog.act:
                        f(a)
            if prog.dve:
                @block.vector
                def _(v, prog=prog):
                    for f in prog.dve:
                        f(v)
            if prog.sp:
                @block.sync
                def _(s, prog=prog):
                    for f in prog.sp:
                        f(s)
            if prog.gp:
                @block.gpsimd
                def _(g, prog=prog):
                    for f in prog.gp:
                        f(g)

    es.close()
    return nc


def _build_ex4():
    r = np.arange(128)
    m = np.arange(128)
    ex = np.zeros((512, 2048), dtype=np.float32)
    for v in range(4):
        for hh in range(2):
            for k in range(8):
                tgt = hh * 16 + 2 * k + (m >= 64).astype(np.int64)
                blk = ((r[:, None] % 32 == tgt[None, :]) & (r[:, None] // 32 == v))
                ex[v * 128: v * 128 + 128, (hh * 8 + k) * 128: (hh * 8 + k) * 128 + 128] = blk
    return np.ascontiguousarray(ex)


def kernel(**inputs):
    x = np.ascontiguousarray(np.asarray(inputs["L0_attn_x"], dtype=np.float32))
    assert x.shape == (16, 2048, 1024)

    def T(a):
        return np.ascontiguousarray(np.asarray(a, dtype=np.float32).T)

    def C(a):
        return np.ascontiguousarray(np.asarray(a, dtype=np.float32))

    com = {
        "wqT": T(inputs["L0_attn_Wq"]), "wkT": T(inputs["L0_attn_Wk"]),
        "wvT": T(inputs["L0_attn_Wv"]), "wfcT": T(inputs["L0_attn_Wfc"]),
        "cw1": C(inputs["L0_ffn_cn_w1"]), "cw2": C(inputs["L0_ffn_cn_w2"]),
        "cw3": C(inputs["L0_ffn_cn_w3"]), "cw4": C(inputs["L0_ffn_cn_w4"]),
        "cb1": C(inputs["L0_ffn_cn_b1"]), "cb2": C(inputs["L0_ffn_cn_b2"]),
        "cb3": C(inputs["L0_ffn_cn_b3"]), "cb4": C(inputs["L0_ffn_cn_b4"]),
        "gw1": C(inputs["L0_ffn_gc_w1"]), "gw2": C(inputs["L0_ffn_gc_w2"]),
        "gw3": C(inputs["L0_ffn_gc_w3"]),
        "gb1": C(inputs["L0_ffn_gc_b1"]), "gb2": C(inputs["L0_ffn_gc_b2"]),
        "gb3": C(inputs["L0_ffn_gc_b3"]),
        "ar": C(inputs["L0_ffn_adaptive_range"]),
        "ex4": _build_ex4(),
    }
    nc = build_nc()
    in_maps = []
    for c in range(NCORES):
        m = dict(com)
        m["x"] = np.ascontiguousarray(x[2 * c: 2 * c + 2].reshape(2 * 2048, 1024))
        in_maps.append(m)
    res = run_bass_kernel_spmd(nc, in_maps, core_ids=list(range(NCORES)))
    out = np.empty((16, 2048, 1024), dtype=np.float32)
    for c in range(NCORES):
        out[2 * c: 2 * c + 2] = res.results[c]["out"].reshape(2, 2048, 1024)
    return out
